# version 05
# baseline (speedup 1.0000x reference)
import math
import contextlib
import numpy as np
import ml_dtypes
import concourse.bass as bass
import concourse.mybir as mybir
from concourse.bass_utils import run_bass_kernel_spmd

F32 = mybir.dt.float32
BF16 = mybir.dt.bfloat16
I8 = mybir.dt.int8
AF = mybir.ActivationFunctionType
ALU = mybir.AluOpType
AX = mybir.AxisListType

P = 128
B = 2
S = 2048
HID = 1024
NT = 32
TPB = 16
NH = 16
HD = 64
NPAIR = 8
NKP = 16
LN32 = math.log(32.0 + 1e-9)

ENGINES = ["sync", "scalar", "vector", "tensor", "gpsimd"]


class Prog:
    def __init__(self, nc):
        self.nc = nc
        self.ops = {e: [] for e in ENGINES}
        self.counts = {}
        self.handles = {}
        self.owner = {}

    def sem(self, name, owner):
        self.counts[name] = 0
        self.owner[name] = owner
        return name

    def now(self, name):
        return self.counts[name]

    def emit(self, eng, fn, waits=(), inc=None, inc_n=1):
        if inc is not None:
            assert self.owner[inc] == eng, (inc, eng)
            self.counts[inc] += inc_n
        self.ops[eng].append((fn, tuple(waits), inc, inc_n))
        return self.counts[inc] if inc is not None else None

    def dma(self, eng, dst, src, waits=(), inc=None, slow=False):
        return self.emit(eng, (lambda d, s, sl: (lambda e: e.dma_start(
            out=d, in_=s, allow_slow_non_contiguous=sl)))(dst, src, slow),
            waits=waits, inc=inc, inc_n=16)

    def build(self, stack):
        nc = self.nc
        for name in self.counts:
            self.handles[name] = stack.enter_context(nc.semaphore(name))
        with nc.Block() as block:
            def mk(e):
                def run(eng):
                    for fn, waits, inc, inc_n in self.ops[e]:
                        for s, v in waits:
                            if v > 0:
                                eng.wait_ge(self.handles[s], v)
                        ins = fn(eng)
                        if inc is not None:
                            ins.then_inc(self.handles[inc], inc_n)
                return run
            for e in ENGINES:
                if self.ops[e]:
                    getattr(block, e)(mk(e))


def build_nc(stages=("f1", "a0", "a1", "f2"), debug=False):
    nc = bass.Bass("TRN2", target_bir_lowering=False)

    io = {}
    io["x"] = nc.dram_tensor("x", [B * S, HID], F32, kind="ExternalInput").ap()
    io["wq"] = nc.dram_tensor("wq", [HID, HID], F32, kind="ExternalInput").ap()
    io["wk"] = nc.dram_tensor("wk", [256, HID], F32, kind="ExternalInput").ap()
    io["wv"] = nc.dram_tensor("wv", [256, HID], F32, kind="ExternalInput").ap()
    io["wo"] = nc.dram_tensor("wo", [HID, HID], F32, kind="ExternalInput").ap()
    io["wkgen"] = nc.dram_tensor("wkgen", [3, HID], F32, kind="ExternalInput").ap()
    io["bkgen"] = nc.dram_tensor("bkgen", [1, 3], F32, kind="ExternalInput").ap()
    io["wconv"] = nc.dram_tensor("wconv", [HID, 3], F32, kind="ExternalInput").ap()
    io["bconv"] = nc.dram_tensor("bconv", [1, HID], F32, kind="ExternalInput").ap()
    io["wffn"] = nc.dram_tensor("wffn", [HID, HID], F32, kind="ExternalInput").ap()
    io["bffn"] = nc.dram_tensor("bffn", [1, HID], F32, kind="ExternalInput").ap()
    io["y"] = nc.dram_tensor("y", [B * S, HID], F32, kind="ExternalOutput").ap()

    cc_in = [nc.dram_tensor(f"cc_in{i}", [1, 8], F32) for i in range(2)]
    cc_out = [nc.dram_tensor(f"cc_out{i}", [1, 8], F32) for i in range(2)]
    aspill = nc.dram_tensor("aspill", [B * NPAIR * P, S], BF16)

    p = Prog(nc)
    LD = p.sem("LD", "sync")
    GD = p.sem("GD", "gpsimd")
    TS = p.sem("TS", "tensor")
    VS = p.sem("VS", "vector")
    SS = p.sem("SS", "scalar")

    stack = contextlib.ExitStack()
    sb = lambda name, shape, dt: stack.enter_context(nc.sbuf_tensor(name, shape, dt))
    ps = lambda name, shape, dt: stack.enter_context(nc.psum_tensor(name, shape, dt))

    x_sb = sb("x_sb", [P, NT * HID], F32)
    ufm = sb("ufm", [P, 8 * S], BF16)
    pool = sb("pool", [P, 26624], mybir.dt.uint8)
    bhi = sb("bhi", [P, S], BF16)
    kdup = sb("kdup", [P, S], BF16)
    khat = sb("khat", [66, S], BF16)
    vkt = sb("vkt", [P, NKP * HD], BF16)
    qpair = sb("qpair", [P, S], BF16)
    maskE = sb("maskE", [P, 8 * 32], BF16)
    statm = sb("statm", [P, NT], F32)
    statr = sb("statr", [P, NT], F32)
    stata = sb("stata", [P, NT], F32)
    statb = sb("statb", [P, NT], F32)
    statz = sb("statz", [P, NT], F32)
    ident = sb("ident", [P, P], F32)
    identb = sb("identb", [P, P], BF16)
    ones_r = sb("ones_r", [1, P], F32)
    ones_rb = sb("ones_rb", [1, P], BF16)
    ind64 = sb("ind64", [64, 1], BF16)
    indAB = sb("indAB", [P, 2], BF16)
    sc = sb("sc", [P, 32], F32)
    sc1 = sb("sc1", [1, 32], F32)
    stage = None
    convw = sb("convw", [P, 8 * 3], F32)
    convb = sb("convb", [P, 8], F32)
    ctxa = sb("ctxa", [P, 8], F32)
    ctxb = sb("ctxb", [P, 8], BF16)
    wkg = sb("wkg", [P, 8 * 3], BF16)
    gvec = sb("gvec", [P, 2], F32)
    ebias = sb("ebias", [P, 4], F32)
    ln32t = sb("ln32t", [P, 1], F32)
    padt = sb("padt", [P, 64], F32)

    vfmt = bhi[0:64, :]
    UF32 = ufm.bitcast(F32)
    UB16 = ufm
    BROW = UF32[0:1, 4096:5120]
    BROWB = UB16[0:1, 10240:11264]

    poolb = pool.bitcast(BF16)
    poolf = pool.bitcast(F32)
    tmpv = poolf[:, 4096:4128]
    tmpv2 = poolf[:, 4128:4160]

    WT = poolb[:, 0:8192]
    WTv = lambda: WT.rearrange("p (c n) -> p c n", c=8, n=1024)
    FSTG = poolf[:, 4096:5120]
    QITM = poolb[:, 10240:11264]
    QIFM = poolb[:, 11264:12288]
    QI8 = pool.bitcast(I8)[:, 24576:25600]
    Z2R = [poolb[:, 0:4112], poolb[:, 4112:8224]]
    z2v = lambda i: Z2R[i].rearrange("p (c n) -> p c n", c=8, n=514)
    Z2TM = poolf[:, 4112 // 2 + 2056:0] if False else poolf[:, 4112:5136]
    CVT0 = poolf[:, 5136:5648]
    CVT1 = poolf[:, 5648:6160]
    QHAT = [poolb[0:66, 0:2048], poolb[0:66, 2048:4096],
            poolb[0:66, 4096:6144], poolb[0:66, 6144:8192]]
    E1A = poolb[:, 8192:9216]
    E1B = poolb[:, 9216:10240]
    ETL = [poolb[:, 10240:11264], poolb[:, 11264:12288]]
    ASTG = poolb[:, 12288:13312]
    wslice = ASTG
    LNE = poolf[:, 4096:5120]
    QSQ = poolb[:, 8192:10240]
    OSTG = poolb[:, 8192:12288]
    ufmv = lambda: ufm.rearrange("p (c n) -> p c n", c=8, n=S)
    convwv = convw.rearrange("p (c k) -> p c k", c=8, k=3)

    psA = ps("psA", [P, 1024], F32)
    psB = ps("psB", [P, 1024], F32)
    psC = ps("psC", [P, 1024], F32)
    psD = ps("psD", [P, 1024], F32)
    PSLOT = [psA, psB, psC]
    psTb = psD.bitcast(BF16)
    psCb = psC.bitcast(BF16)

    def T(fn, waits=()):
        return p.emit("tensor", fn, waits=waits, inc=TS)

    def V(fn, waits=()):
        return p.emit("vector", fn, waits=waits, inc=VS)

    def SC(fn, waits=()):
        return p.emit("scalar", fn, waits=waits, inc=SS)

    def G(fn, waits=()):
        return p.emit("gpsimd", fn, waits=waits, inc=GD, inc_n=1)

    def pad():
        V(lambda e: e.memset(padt[:, :], 0.0))
        return V(lambda e: e.memset(padt[:, :], 1.0))

    def bcast128(dst_col, src11, wait_v, psum_slice, wait_s=0):
        t = T(lambda e, s=src11, o=psum_slice: e.matmul(
            o, ones_r[0:1, :], s, start=True, stop=True, skip_group_check=True),
            waits=[(VS, wait_v), (SS, wait_s)])
        V(lambda e, o=psum_slice, d=dst_col: e.tensor_copy(out=d, in_=o),
          waits=[(TS, t)])
        return pad()

    xt = lambda t: x_sb[:, t * HID:(t + 1) * HID]
    xio = io["x"].rearrange("(n p) h -> n p h", p=P)
    yio = io["y"].rearrange("(n p) h -> n p h", p=P)
    for t in range(NT):
        p.dma("sync", xt(t), xio[t], inc=LD)
    LD_X = p.now("LD")

    p.dma("sync", convw.rearrange("p (c k) -> p c k", c=8, k=3),
          io["wconv"].rearrange("(c p) k -> p c k", p=P), inc=LD)
    p.dma("sync", convb.rearrange("p (c o) -> p c o", c=8, o=1),
          io["bconv"].rearrange("o (c p) -> p c o", p=P), inc=LD, slow=True)
    p.dma("sync", FSTG[0:3, 0:1024], io["wkgen"], inc=LD)
    LD_WKG = p.now("LD")

    G(lambda e: e.memset(ident[:, :], 1.0))
    g_id = G(lambda e: e.affine_select(ident[:, :], ident[:, :], [[1, P]], ALU.is_equal, 0.0,
                                       base=0, channel_multiplier=-1))
    v_idb = V(lambda e: e.tensor_copy(out=identb[:, :], in_=ident[:, :]), waits=[(GD, g_id)])
    v_wkgc = V(lambda e: e.tensor_copy(out=QITM[0:3, 0:1024], in_=FSTG[0:3, 0:1024]),
               waits=[(LD, LD_WKG)])
    t_wkg = None
    for c in range(8):
        t_wkg = T(lambda e, c=c: e.transpose(psTb[:, c * 4:c * 4 + 3],
                                             QITM[0:3, c * P:(c + 1) * P], identb[0:3, 0:3]),
                  waits=[(VS, max(v_idb, v_wkgc))] if c == 0 else [])
    v_wkg = V(lambda e: e.tensor_copy(
        out=wkg.rearrange("p (c k) -> p c k", c=8, k=3),
        in_=psTb[:, 0:32].rearrange("p (c k) -> p c k", c=8, k=4)[:, :, 0:3]),
        waits=[(TS, t_wkg)])
    G(lambda e: e.memset(ones_r[:, :], 1.0))
    G(lambda e: e.memset(ones_rb[:, :], 1.0))
    G(lambda e: e.memset(ind64[:, :], 1.0))
    G(lambda e: e.memset(indAB[:, :], 0.0))
    G(lambda e: e.memset(indAB[0:64, 0:1], 1.0))
    G(lambda e: e.memset(indAB[64:128, 1:2], 1.0))
    G(lambda e: e.memset(ln32t[:, :], LN32))
    G(lambda e: e.memset(maskE[:, :], 0.0))
    for j in range(8):
        G(lambda e, j=j: e.memset(maskE[0:64, j * 32 + 2 * j: j * 32 + 2 * j + 1], 1.0))
        G(lambda e, j=j: e.memset(maskE[64:128, j * 32 + 2 * j + 1: j * 32 + 2 * j + 2], 1.0))
    G_CONST = p.now("GD")

    def ffn_pass(idx):
        v_i = V(lambda e: e.memset(sc[:, 0:1], 0.0))
        for r in range(8):
            l = p.dma("sync", FSTG[:, :], io["wffn"][r * P:(r + 1) * P, :],
                      waits=([(VS, v_wkg)] if (idx == 0 and r == 0) else
                             ([(VS, p.now("VS"))] if r == 0 else [])) +
                            [(SS, p.now("SS"))], inc=LD)
            s_abs = SC(lambda e: e.activation(UF32[:, 0:1024], FSTG[:, :], AF.Abs,
                                              bias=0.0, scale=1.0),
                       waits=[(LD, l), (VS, p.now("VS"))])
            v_r = V(lambda e: e.tensor_reduce(out=sc[:, 1:2], in_=UF32[:, 0:1024], axis=AX.X,
                                              op=ALU.add), waits=[(SS, s_abs)])
            pad()
            v_i = V(lambda e: e.tensor_tensor(out=sc[:, 0:1], in0=sc[:, 0:1], in1=sc[:, 1:2],
                                              op=ALU.add))
        t_tr = T(lambda e: e.transpose(psC[0:1, 0:P], sc[:, 0:1], ident[:, :]),
                 waits=[(VS, v_i), (GD, G_CONST)])
        v_s = V(lambda e: e.tensor_reduce(out=sc1[0:1, 0:1], in_=psC[0:1, 0:P], axis=AX.X,
                                          op=ALU.add), waits=[(TS, t_tr)])
        pad()
        v_thr = V(lambda e: e.tensor_scalar(out=sc1[0:1, 1:2], in0=sc1[0:1, 0:1],
                                            scalar1=0.5 / (1024.0 * 1024.0), scalar2=None,
                                            op0=ALU.mult, op1=ALU.bypass))
        v_ws = V(lambda e: e.tensor_scalar(out=sc1[0:1, 2:3], in0=sc1[0:1, 0:1],
                                           scalar1=1.0 / (1024.0 * 1024.0), scalar2=None,
                                           op0=ALU.mult, op1=ALU.bypass))
        v_thrb = bcast128(sc[:, 2:3], sc1[0:1, 1:2], v_thr, psC[:, 512:513])
        v_thrn = V(lambda e: e.tensor_scalar(out=sc[:, 3:4], in0=sc[:, 2:3], scalar1=-1.0,
                                             scalar2=None, op0=ALU.mult, op1=ALU.bypass))
        for r in range(8):
            l = p.dma("sync", FSTG[:, :], io["wffn"][r * P:(r + 1) * P, :],
                      waits=([(SS, p.now("SS"))] if r == 0 else []) + [(VS, p.now("VS"))],
                      inc=LD)
            v1 = V(lambda e: e.tensor_scalar(out=UF32[:, 0:1024], in0=FSTG[:, :],
                                             scalar1=sc[:, 2:3], scalar2=None,
                                             op0=ALU.is_gt, op1=ALU.bypass),
                   waits=[(LD, l), (VS, v_thrn)])
            v2 = V(lambda e: e.tensor_scalar(out=UF32[:, 1024:2048], in0=FSTG[:, :],
                                             scalar1=sc[:, 3:4], scalar2=None,
                                             op0=ALU.is_lt, op1=ALU.bypass))
            v3 = V(lambda e: e.tensor_tensor(out=QITM[:, :], in0=UF32[:, 0:1024],
                                             in1=UF32[:, 1024:2048], op=ALU.subtract))
            tt = None
            for fc in range(8):
                tt = T(lambda e, fc=fc: e.transpose(psTb[:, fc * P:(fc + 1) * P],
                                                    QITM[:, fc * P:(fc + 1) * P], identb[:, :]),
                       waits=[(VS, max(v3, v_idb)), (VS, p.now("VS"))] if fc == 0 else [])
            V(lambda e, r=r: e.tensor_copy(
                out=WTv()[:, :, r * P:(r + 1) * P],
                in_=psTb[:, 0:1024].rearrange("p (c n) -> p c n", c=8, n=P)),
                waits=[(TS, tt)])
        for t in range(NT):
            x_t = xt(t)
            va = V(lambda e, x_t=x_t: e.tensor_tensor(out=UF32[:, 2048:3072], in0=x_t, in1=x_t,
                                                      op=ALU.mult),
                   waits=[(LD, LD_X)] if t == 0 else [])
            V(lambda e, t=t, x_t=x_t: e.tensor_reduce(out=statm[:, t:t + 1], in_=x_t,
                                                      axis=AX.X, op=ALU.add))
            V(lambda e, t=t: e.tensor_reduce(out=statr[:, t:t + 1], in_=UF32[:, 2048:3072],
                                             axis=AX.X, op=ALU.add))
            V(lambda e, t=t, x_t=x_t: e.tensor_reduce(out=tmpv[:, t:t + 1], in_=x_t,
                                                      axis=AX.X, op=ALU.max))
            V(lambda e, t=t, x_t=x_t: e.tensor_reduce(out=tmpv2[:, t:t + 1], in_=x_t,
                                                      axis=AX.X, op=ALU.min))
        V(lambda e: e.tensor_scalar(out=statm[:, :], in0=statm[:, :], scalar1=1.0 / HID,
                                    scalar2=None, op0=ALU.mult, op1=ALU.bypass))
        V(lambda e: e.tensor_scalar(out=statr[:, :], in0=statr[:, :], scalar1=1.0 / HID,
                                    scalar2=None, op0=ALU.mult, op1=ALU.bypass))
        pad()
        V(lambda e: e.tensor_tensor(out=statz[:, :], in0=statm[:, :], in1=statm[:, :],
                                    op=ALU.mult))
        pad()
        V(lambda e: e.tensor_tensor(out=statr[:, :], in0=statr[:, :], in1=statz[:, :],
                                    op=ALU.subtract))
        pad()
        V(lambda e: e.tensor_scalar(out=statr[:, :], in0=statr[:, :], scalar1=1.0,
                                    scalar2=1e-5, op0=ALU.mult, op1=ALU.add))
        pad()
        vr0 = V(lambda e: e.reciprocal(out=statz[:, :], in_=statr[:, :]))
        vr0 = pad()
        ss0 = SC(lambda e: e.activation(stata[:, :], statz[:, :], AF.Sqrt, bias=0.0, scale=1.0),
                 waits=[(VS, vr0)])
        V(lambda e: e.tensor_tensor(out=statb[:, :], in0=stata[:, :], in1=stata[:, :],
                                    op=ALU.mult), waits=[(SS, ss0)])
        pad()
        V(lambda e: e.tensor_tensor(out=statb[:, :], in0=statb[:, :], in1=statr[:, :],
                                    op=ALU.mult))
        pad()
        V(lambda e: e.tensor_scalar(out=statb[:, :], in0=statb[:, :], scalar1=-0.5,
                                    scalar2=1.5, op0=ALU.mult, op1=ALU.add))
        pad()
        V(lambda e: e.tensor_tensor(out=statr[:, :], in0=stata[:, :], in1=statb[:, :],
                                    op=ALU.mult))
        pad()
        V(lambda e: e.tensor_tensor(out=tmpv[:, :], in0=tmpv[:, :], in1=statm[:, :],
                                    op=ALU.subtract))
        V(lambda e: e.tensor_tensor(out=tmpv2[:, :], in0=statm[:, :], in1=tmpv2[:, :],
                                    op=ALU.subtract))
        pad()
        V(lambda e: e.tensor_tensor(out=tmpv[:, :], in0=tmpv[:, :], in1=tmpv2[:, :],
                                    op=ALU.max))
        pad()
        V(lambda e: e.tensor_tensor(out=statz[:, :], in0=tmpv[:, :], in1=statr[:, :],
                                    op=ALU.mult))
        pad()
        vz5 = V(lambda e: e.tensor_reduce(out=sc[:, 4:5], in_=statz[:, :], axis=AX.X, op=ALU.max))
        tz = T(lambda e: e.transpose(psC[0:1, 0:P], sc[:, 4:5], ident[:, :]), waits=[(VS, vz5)])
        vz6 = V(lambda e: e.tensor_reduce(out=sc1[0:1, 4:5], in_=psC[0:1, 0:P], axis=AX.X,
                                          op=ALU.max), waits=[(TS, tz)])
        vz6 = pad()
        ts8 = T(lambda e: e.matmul(psC[0:1, 512:520], sc1[0:1, 4:5], ones_r[0:1, 0:8],
                                   start=True, stop=True, skip_group_check=True),
                waits=[(VS, vz6)])
        vs8 = V(lambda e: e.tensor_copy(out=sc1[0:1, 16:24], in_=psC[0:1, 512:520]), waits=[(TS, ts8)])
        g1 = p.dma("gpsimd", cc_in[idx].ap(), sc1[0:1, 16:24], waits=[(VS, vs8)], inc=GD)
        g2 = G(lambda e: e.collective_compute(
            "AllReduce", mybir.AluOpType.max, replica_groups=[list(range(8))],
            ins=[cc_in[idx].ap()], outs=[cc_out[idx].ap()]), waits=[(GD, g1)])
        g3 = p.dma("gpsimd", sc1[0:1, 5:6], cc_out[idx].ap()[0:1, 0:1], waits=[(GD, g2)], inc=GD)
        V(lambda e: e.tensor_scalar(out=sc1[0:1, 6:7], in0=sc1[0:1, 5:6], scalar1=1.0 / 127.0,
                                    scalar2=None, op0=ALU.mult, op1=ALU.bypass),
          waits=[(GD, g3)])
        pad()
        vi2 = V(lambda e: e.reciprocal(out=sc1[0:1, 7:8], in_=sc1[0:1, 6:7]))
        pad()
        vi3 = V(lambda e: e.tensor_tensor(out=sc1[0:1, 8:9], in0=sc1[0:1, 2:3],
                                          in1=sc1[0:1, 6:7], op=ALU.mult))
        pad()
        vi4 = V(lambda e: e.tensor_scalar(out=sc1[0:1, 9:10], in0=sc1[0:1, 8:9], scalar1=0.5,
                                          scalar2=None, op0=ALU.mult, op1=ALU.bypass))
        vi5 = V(lambda e: e.reciprocal(out=sc1[0:1, 10:11], in_=sc1[0:1, 8:9]))
        vi5 = pad()
        lb = p.dma("sync", BROW, io["bffn"], waits=[(SS, p.now("SS"))], inc=LD)
        vi6 = V(lambda e: e.tensor_scalar(out=BROWB, in0=BROW, scalar1=sc1[0:1, 10:11],
                                          scalar2=None, op0=ALU.mult, op1=ALU.bypass),
                waits=[(LD, lb), (VS, vi5)])
        v_ri = bcast128(sc[:, 5:6], sc1[0:1, 7:8], vi2, psC[:, 520:521])
        v_w2 = bcast128(sc[:, 6:7], sc1[0:1, 9:10], vi4, psC[:, 521:522])
        V(lambda e: e.tensor_scalar(out=stata[:, :], in0=statr[:, :], scalar1=sc[:, 5:6],
                                    scalar2=None, op0=ALU.mult, op1=ALU.bypass),
          waits=[(VS, v_ri)])
        pad()
        V(lambda e: e.tensor_tensor(out=statb[:, :], in0=statm[:, :], in1=stata[:, :],
                                    op=ALU.mult))
        pad()
        V(lambda e: e.tensor_scalar(out=statb[:, :], in0=statb[:, :], scalar1=-1.0,
                                    scalar2=None, op0=ALU.mult, op1=ALU.bypass))
        va3 = pad()
        for t in range(NT):
            x_t = xt(t)
            out_ps = psA if t % 2 == 0 else psB
            vq1 = V(lambda e, t=t, x_t=x_t: e.tensor_scalar(
                out=QI8, in0=x_t, scalar1=stata[:, t:t + 1], scalar2=statb[:, t:t + 1],
                op0=ALU.mult, op1=ALU.add), waits=[(VS, va3)])
            vq2 = V(lambda e: e.tensor_copy(out=QITM[:, :], in_=QI8))
            tt = None
            for fc in range(8):
                tt = T(lambda e, fc=fc: e.transpose(psTb[:, 1024 + fc * P:1024 + (fc + 1) * P],
                                                    QITM[:, fc * P:(fc + 1) * P], identb[:, :]),
                       waits=[(VS, vq2)] if fc == 0 else [])
            vq3 = V(lambda e: e.tensor_copy(out=QIFM[:, :], in_=psTb[:, 1024:2048]),
                    waits=[(TS, tt)])
            tm = T(lambda e, o=out_ps: e.matmul(o[:, 0:512], ones_rb[0:1, :], BROWB[0:1, 0:512],
                                                start=True, stop=False, skip_group_check=True),
                   waits=[(VS, max(vq3, vi6))])
            tm = T(lambda e, o=out_ps: e.matmul(o[:, 512:1024], ones_rb[0:1, :],
                                                BROWB[0:1, 512:1024],
                                                start=True, stop=False, skip_group_check=True))
            for fc in range(8):
                for half in range(2):
                    tm = T(lambda e, fc=fc, half=half, o=out_ps: e.matmul(
                        o[:, half * 512:(half + 1) * 512],
                        QIFM[:, fc * P:(fc + 1) * P],
                        WTv()[:, fc, half * 512:(half + 1) * 512],
                        start=False, stop=(fc == 7), skip_group_check=True))
            V(lambda e, o=out_ps: e.tensor_scalar(
                out=UF32[:, 3072:4096], in0=o[:, :], scalar1=sc[:, 6:7], scalar2=None,
                op0=ALU.mult, op1=ALU.bypass), waits=[(TS, tm), (VS, v_w2)])
            V(lambda e, x_t=x_t: e.tensor_tensor(out=x_t, in0=x_t, in1=UF32[:, 3072:4096],
                                                 op=ALU.add))

    def attention(b):
        sl = slice(TPB * b, TPB * (b + 1))
        for i, t in enumerate(range(TPB * b, TPB * (b + 1))):
            x_t = xt(t)
            V(lambda e, x_t=x_t: e.tensor_tensor(out=Z2TM[:, :], in0=x_t, in1=x_t, op=ALU.mult),
              waits=[(VS, p.now("VS"))] if i == 0 else [])
            V(lambda e, t=t, x_t=x_t: e.tensor_reduce(out=statm[:, t:t + 1], in_=x_t,
                                                      axis=AX.X, op=ALU.add))
            V(lambda e, t=t: e.tensor_reduce(out=statr[:, t:t + 1], in_=Z2TM[:, :],
                                             axis=AX.X, op=ALU.add))
        V(lambda e: e.tensor_scalar(out=statm[:, sl], in0=statm[:, sl], scalar1=1.0 / HID,
                                    scalar2=None, op0=ALU.mult, op1=ALU.bypass))
        V(lambda e: e.tensor_scalar(out=statr[:, sl], in0=statr[:, sl], scalar1=1.0 / HID,
                                    scalar2=None, op0=ALU.mult, op1=ALU.bypass))
        pad()
        V(lambda e: e.tensor_tensor(out=statz[:, sl], in0=statm[:, sl], in1=statm[:, sl],
                                    op=ALU.mult))
        pad()
        V(lambda e: e.tensor_tensor(out=statr[:, sl], in0=statr[:, sl], in1=statz[:, sl],
                                    op=ALU.subtract))
        pad()
        V(lambda e: e.tensor_scalar(out=statr[:, sl], in0=statr[:, sl], scalar1=1.0,
                                    scalar2=1e-5, op0=ALU.mult, op1=ALU.add))
        pad()
        vr0 = V(lambda e: e.reciprocal(out=statz[:, sl], in_=statr[:, sl]))
        vr0 = pad()
        ss0 = SC(lambda e: e.activation(stata[:, sl], statz[:, sl], AF.Sqrt, bias=0.0, scale=1.0),
                 waits=[(VS, vr0)])
        V(lambda e: e.tensor_tensor(out=statb[:, sl], in0=stata[:, sl], in1=stata[:, sl],
                                    op=ALU.mult), waits=[(SS, ss0)])
        pad()
        V(lambda e: e.tensor_tensor(out=statb[:, sl], in0=statb[:, sl], in1=statr[:, sl],
                                    op=ALU.mult))
        pad()
        V(lambda e: e.tensor_scalar(out=statb[:, sl], in0=statb[:, sl], scalar1=-0.5,
                                    scalar2=1.5, op0=ALU.mult, op1=ALU.add))
        pad()
        V(lambda e: e.tensor_tensor(out=statr[:, sl], in0=stata[:, sl], in1=statb[:, sl],
                                    op=ALU.mult))
        pad()
        V(lambda e: e.tensor_tensor(out=statb[:, sl], in0=statm[:, sl], in1=statr[:, sl],
                                    op=ALU.mult))
        pad()
        V(lambda e: e.tensor_scalar(out=statb[:, sl], in0=statb[:, sl], scalar1=-1.0,
                                    scalar2=None, op0=ALU.mult, op1=ALU.bypass))
        vmb2 = pad()
        V(lambda e: e.memset(ctxa[:, :], 0.0))

        def produce_z2(j):
            slot = z2v(j % 2)
            for q in range(4):
                t = TPB * b + 4 * j + q
                x_t = xt(t)
                va = V(lambda e, t=t, x_t=x_t: e.tensor_scalar(
                    out=Z2TM[:, :], in0=x_t, scalar1=statr[:, t:t + 1], scalar2=statb[:, t:t + 1],
                    op0=ALU.mult, op1=ALU.add))
                tt = None
                for fc in range(8):
                    dst = psA if fc < 4 else psB
                    tt = T(lambda e, fc=fc, dst=dst: e.transpose(
                        dst[:, (fc % 4) * P:(fc % 4 + 1) * P], Z2TM[:, fc * P:(fc + 1) * P],
                        ident[:, :]),
                        waits=[(VS, p.now("VS"))] if fc == 0 else [])
                V(lambda e, q=q, slot=slot: e.tensor_copy(
                    out=slot[:, 0:4, 1 + q * P:1 + (q + 1) * P],
                    in_=psA[:, 0:512].rearrange("p (c n) -> p c n", c=4, n=P)), waits=[(TS, tt)])
                V(lambda e, q=q, slot=slot: e.tensor_copy(
                    out=slot[:, 4:8, 1 + q * P:1 + (q + 1) * P],
                    in_=psB[:, 0:512].rearrange("p (c n) -> p c n", c=4, n=P)))
            V(lambda e, slot=slot: e.tensor_reduce(out=sc[:, 16:24], in_=slot[:, :, 1:513],
                                                   axis=AX.X, op=ALU.add))
            pad()
            V(lambda e: e.tensor_tensor(out=ctxa[:, :], in0=ctxa[:, :], in1=sc[:, 16:24],
                                        op=ALU.add))
            pad()
            if j == 0:
                V(lambda e, slot=slot: e.memset(slot[:, :, 0:1], 0.0))
            else:
                prev = z2v((j - 1) % 2)
                V(lambda e, slot=slot, prev=prev: e.tensor_copy(
                    out=slot[:, :, 0:1], in_=prev[:, :, 512:513]))
                V(lambda e, slot=slot, prev=prev: e.tensor_copy(
                    out=prev[:, :, 513:514], in_=slot[:, :, 1:2]))
            if j == 3:
                V(lambda e, slot=slot: e.memset(slot[:, :, 513:514], 0.0))

        def conv_block(j):
            slot = z2v(j % 2)
            for fc in range(8):
                zc = slot[:, fc]
                w0, w1, w2 = (convwv[:, fc, k:k + 1] for k in range(3))
                bcv = convb[:, fc:fc + 1]
                V(lambda e, zc=zc, w1=w1, bcv=bcv: e.tensor_scalar(
                    out=CVT0[:, :], in0=zc[:, 1:513], scalar1=w1, scalar2=bcv,
                    op0=ALU.mult, op1=ALU.add))
                V(lambda e, zc=zc, w0=w0: e.tensor_scalar(
                    out=CVT1[:, :], in0=zc[:, 0:512], scalar1=w0, scalar2=None,
                    op0=ALU.mult, op1=ALU.bypass))
                V(lambda e: e.tensor_tensor(out=CVT0[:, :], in0=CVT0[:, :], in1=CVT1[:, :],
                                            op=ALU.add))
                V(lambda e, zc=zc, w2=w2: e.tensor_scalar(
                    out=CVT1[:, :], in0=zc[:, 2:514], scalar1=w2, scalar2=None,
                    op0=ALU.mult, op1=ALU.bypass))
                V(lambda e, fc=fc, j=j: e.tensor_tensor(
                    out=ufmv()[:, fc, j * 512:(j + 1) * 512],
                    in0=CVT0[:, :], in1=CVT1[:, :], op=ALU.add))

        produce_z2(0)
        for j in range(4):
            if j < 3:
                produce_z2(j + 1)
            conv_block(j)
        V_U = p.now("VS")
        vg1 = V(lambda e: e.tensor_scalar(out=ctxb[:, :], in0=ctxa[:, :], scalar1=1.0 / S,
                                          scalar2=None, op0=ALU.mult, op1=ALU.bypass))
        tg = None
        for c in range(8):
            tg = T(lambda e, c=c: e.matmul(
                psC[0:1, 640:643], ctxb[:, c:c + 1], wkg.rearrange("p (c k) -> p c k", c=8, k=3)[:, c],
                start=(c == 0), stop=(c == 7), skip_group_check=True),
                waits=[(VS, vg1)] if c == 0 else [])
        lg = p.dma("sync", sc1[0:1, 12:15], io["bkgen"], inc=LD)
        vg2 = V(lambda e: e.tensor_copy(out=sc1[0:1, 16:19], in_=psC[0:1, 640:643]),
                waits=[(TS, tg)])
        pad()
        vg3 = V(lambda e: e.tensor_tensor(out=sc1[0:1, 16:19], in0=sc1[0:1, 16:19],
                                          in1=sc1[0:1, 12:15], op=ALU.add), waits=[(LD, lg)])
        sg1 = SC(lambda e: e.activation(sc1[0:1, 20:23], sc1[0:1, 16:19], AF.Sigmoid,
                                        bias=0.0, scale=1.0), waits=[(VS, vg3)])
        vg4 = V(lambda e: e.tensor_reduce(out=sc1[0:1, 24:25], in_=sc1[0:1, 20:23], axis=AX.X,
                                          op=ALU.add), waits=[(SS, sg1)])
        pad()
        vg5 = V(lambda e: e.tensor_scalar(out=sc1[0:1, 25:26], in0=sc1[0:1, 24:25],
                                          scalar1=1.0 / 24.0, scalar2=None, op0=ALU.mult,
                                          op1=ALU.bypass))
        vg6 = V(lambda e: e.tensor_scalar(out=sc1[0:1, 26:27], in0=sc1[0:1, 24:25],
                                          scalar1=1.0 / 3.0, scalar2=None, op0=ALU.mult,
                                          op1=ALU.bypass))
        vg6 = pad()
        v_qs = bcast128(gvec[:, 0:1], sc1[0:1, 25:26], vg5, psC[:, 644:645])
        v_ks = bcast128(gvec[:, 1:2], sc1[0:1, 26:27], vg6, psC[:, 645:646])

        def build_wT(w_io, r0, nrows, extra_waits=()):
            l = p.dma("sync", FSTG[0:nrows, 0:HID], w_io[r0:r0 + nrows, :],
                      waits=[(VS, p.now("VS"))], inc=LD)
            vv = V(lambda e, nrows=nrows: e.tensor_copy(out=QITM[0:nrows, :],
                                                        in_=FSTG[0:nrows, 0:HID]),
                   waits=[(LD, l)] + list(extra_waits))
            tt = None
            for fc in range(8):
                tt = T(lambda e, fc=fc, nrows=nrows: e.transpose(
                    psTb[:, fc * nrows:(fc + 1) * nrows], QITM[0:nrows, fc * P:(fc + 1) * P],
                    identb[0:nrows, 0:nrows]),
                    waits=[(VS, vv), (VS, p.now("VS"))] if fc == 0 else [])
            return V(lambda e, nrows=nrows: e.tensor_copy(out=wslice[:, 0:8 * nrows],
                                                          in_=psTb[:, 0:8 * nrows]),
                     waits=[(TS, tt), (LD, p.now("LD"))])

        def project(nrows, dst_rows, scale_col, evac_to, wait_w):
            tprev = None
            for tb in range(4):
                dstp = psA if tb % 2 == 0 else psB
                half = (tb // 2) * 512
                for fc in range(8):
                    tprev = T(lambda e, tb=tb, fc=fc, dstp=dstp, half=half, nrows=nrows: e.matmul(
                        dstp[0:nrows, half:half + 512],
                        wslice[:, fc * nrows:(fc + 1) * nrows],
                        ufmv()[:, fc, tb * 512:(tb + 1) * 512],
                        start=(fc == 0), stop=(fc == 7), skip_group_check=True),
                        waits=[(VS, wait_w)] if (tb == 0 and fc == 0) else [])
            vlast = None
            for tb in range(4):
                dstp = psA if tb % 2 == 0 else psB
                half = (tb // 2) * 512
                vlast = V(lambda e, tb=tb, dstp=dstp, half=half: e.tensor_scalar(
                    out=evac_to[dst_rows, tb * 512:(tb + 1) * 512],
                    in0=dstp[0:(dst_rows.stop - dst_rows.start), half:half + 512],
                    scalar1=gvec[0:(dst_rows.stop - dst_rows.start), scale_col:scale_col + 1],
                    scalar2=None, op0=ALU.mult, op1=ALU.bypass),
                    waits=[(TS, tprev), (VS, p.now("VS"))] if tb == 0 else [])
            return vlast

        def build_kv(kv):
            vK = build_wT(io["wk"], kv * 64, 64, extra_waits=[(VS, p.now("VS"))])
            vKd = project(64, slice(0, 64), 1, kdup, vK)
            dK1 = p.dma("sync", kdup[64:128, :], kdup[0:64, :], waits=[(VS, vKd)], inc=LD)
            dK2 = p.dma("sync", khat[0:64, :], kdup[0:64, :], inc=LD)
            vV = build_wT(io["wv"], kv * 64, 64, extra_waits=[(VS, vKd)])
            vV2 = project(64, slice(0, 64), 1, vfmt, vV)
            tv = None
            for kpi in range(NKP):
                tv = T(lambda e, kpi=kpi: e.transpose(
                    psTb[:, kpi * 64:(kpi + 1) * 64], vfmt[0:64, kpi * P:(kpi + 1) * P],
                    identb[0:64, 0:64]),
                    waits=[(VS, vV2), (VS, p.now("VS"))] if kpi == 0 else [])
            vvk = V(lambda e: e.tensor_copy(out=vkt[:, :], in_=psTb[:, 0:NKP * 64]),
                    waits=[(TS, tv)])
            sKq = SC(lambda e: e.activation(QSQ[0:64, :], kdup[0:64, :], AF.Square,
                                            bias=0.0, scale=1.0),
                     waits=[(VS, vKd), (LD, p.now("LD")), (SS, p.now("SS"))])
            vkm = V(lambda e: e.memset(sc1[0:1, 28:29], 0.0))
            for tb in range(4):
                tn = T(lambda e, tb=tb: e.matmul(
                    psC[0:1, 0:512], ind64[:, :], QSQ[0:64, tb * 512:(tb + 1) * 512],
                    start=True, stop=True, skip_group_check=True),
                    waits=[(SS, sKq), (VS, p.now("VS"))] if tb == 0 else [(VS, p.now("VS"))])
                V(lambda e: e.tensor_reduce(out=sc1[0:1, 29:30], in_=psC[0:1, 0:512],
                                            axis=AX.X, op=ALU.max), waits=[(TS, tn)])
                pad()
                V(lambda e: e.tensor_tensor(out=sc1[0:1, 28:29], in0=sc1[0:1, 28:29],
                                            in1=sc1[0:1, 29:30], op=ALU.max))
                vkm = pad()
            V(lambda e: e.memset(QSQ[0:1, 0:S], 0.0), waits=[(TS, p.now("TS"))])
            V(lambda e: e.memset(ETL[0][0:1, 0:1024], 0.0))
            V(lambda e: e.memset(ETL[1][0:1, 0:1024], 0.0))
            vmk = None
            for kpi in range(NKP):
                V(lambda e, kpi=kpi: e.memset(QSQ[0:1, kpi * P:kpi * P + 64], 1.0))
                c0 = kpi * P + 64
                ei, off = (0, c0) if c0 < 1024 else (1, c0 - 1024)
                vmk = V(lambda e, ei=ei, off=off: e.memset(ETL[ei][0:1, off:off + 64], 1.0))
            dm1 = p.dma("sync", khat[64:65, :], QSQ[0:1, 0:S], waits=[(VS, vmk)], inc=LD)
            dm2a = p.dma("sync", khat[65:66, 0:1024], ETL[0][0:1, 0:1024], inc=LD)
            dmask = p.dma("sync", khat[65:66, 1024:2048], ETL[1][0:1, 0:1024], inc=LD)
            return vvk, vkm, dmask

        for pr in range(NPAIR):
            kv = pr // 2
            if pr % 2 == 0:
                vvk, v_km, dmask_kv = build_kv(kv)
            vWq = build_wT(io["wq"], pr * P, P, extra_waits=[(VS, p.now("VS"))])
            vqp = project(P, slice(0, P), 0, qpair, vWq)
            sQq = SC(lambda e: e.activation(QSQ[:, :], qpair[:, :], AF.Square, bias=0.0,
                                            scale=1.0), waits=[(VS, vqp), (SS, p.now("SS"))])
            vqm = V(lambda e: e.memset(sc[0:2, 31:32], 0.0))
            for tb in range(4):
                tn = T(lambda e, tb=tb: e.matmul(
                    psC[0:2, 0:512], indAB[:, :], QSQ[:, tb * 512:(tb + 1) * 512],
                    start=True, stop=True, skip_group_check=True),
                    waits=[(SS, sQq), (VS, p.now("VS"))] if tb == 0 else [(VS, p.now("VS"))])
                V(lambda e: e.tensor_reduce(out=sc[0:2, 30:31], in_=psC[0:2, 0:512],
                                            axis=AX.X, op=ALU.max), waits=[(TS, tn)])
                pad()
                V(lambda e: e.tensor_tensor(out=sc[0:2, 31:32], in0=sc[0:2, 31:32],
                                            in1=sc[0:2, 30:31], op=ALU.max))
                vqm = pad()
            tq = T(lambda e: e.transpose(psC[0:1, 0:P], sc[:, 31:32], ident[:, :]),
                   waits=[(VS, vqm)])
            vM1 = V(lambda e: e.tensor_scalar(out=sc1[0:1, 4:6], in0=psC[0:1, 0:2],
                                              scalar1=sc1[0:1, 28:29], scalar2=None,
                                              op0=ALU.add, op1=ALU.bypass),
                    waits=[(TS, tq), (VS, v_km)])
            pad()
            vM2 = V(lambda e: e.tensor_scalar(out=sc1[0:1, 4:6], in0=sc1[0:1, 4:6],
                                              scalar1=-1.0 / 16.0, scalar2=None,
                                              op0=ALU.mult, op1=ALU.bypass))
            vM2 = pad()
            vb0 = bcast128(ebias[:, 0:1], sc1[0:1, 4:5], vM2, psC[:, 646:647], wait_s=p.now("SS"))
            vb1 = bcast128(ebias[:, 1:2], sc1[0:1, 5:6], vM2, psC[:, 647:648], wait_s=p.now("SS"))
            V(lambda e: e.tensor_tensor(out=ebias[:, 2:3], in0=ebias[:, 0:1],
                                        in1=ln32t[:, :], op=ALU.subtract),
              waits=[(VS, max(vb0, vb1)), (GD, G_CONST)])
            vb3 = V(lambda e: e.tensor_tensor(out=ebias[:, 3:4], in0=ebias[:, 1:2],
                                              in1=ln32t[:, :], op=ALU.subtract))
            vb3 = pad()
            for h01 in range(2):
                src = qpair[64 * h01:64 * (h01 + 1), :]
                for par in range(2):
                    p.dma("sync", QHAT[2 * h01 + par][0:64, :], src, waits=[(VS, vqp)], inc=LD)
            LD_Q = p.now("LD")

            for qh in range(2):
                q0 = qh * 1024
                sLn_prev = p.now("SS")
                for kp in range(NKP):
                    sA = PSLOT[(2 * kp) % 3]
                    sB = PSLOT[(2 * kp + 1) % 3]
                    wa = [(LD, LD_Q), (VS, p.now("VS"))] if kp <= 1 else []
                    T(lambda e, kp=kp, sA=sA, q0=q0: e.matmul(
                        sA[:, 0:512], kdup[0:64, kp * P:(kp + 1) * P], qpair[0:64, q0:q0 + 512],
                        start=True, stop=True, tile_position=(0, 0), skip_group_check=True),
                        waits=wa)
                    t1 = T(lambda e, kp=kp, sA=sA, q0=q0: e.matmul(
                        sA[:, 512:1024], kdup[0:64, kp * P:(kp + 1) * P],
                        qpair[0:64, q0 + 512:q0 + 1024],
                        start=True, stop=True, tile_position=(0, 0), skip_group_check=True))
                    T(lambda e, kp=kp, sB=sB, q0=q0: e.matmul(
                        sB[:, 0:512], kdup[64:128, kp * P:(kp + 1) * P],
                        qpair[64:128, q0:q0 + 512],
                        start=True, stop=True, tile_position=(64, 0), skip_group_check=True))
                    t2 = T(lambda e, kp=kp, sB=sB, q0=q0: e.matmul(
                        sB[:, 512:1024], kdup[64:128, kp * P:(kp + 1) * P],
                        qpair[64:128, q0 + 512:q0 + 1024],
                        start=True, stop=True, tile_position=(64, 0), skip_group_check=True))
                    s1 = SC(lambda e, sA=sA: e.activation(E1A[:, :], sA[:, :], AF.Exp,
                                                          bias=ebias[:, 0:1], scale=1.0),
                            waits=[(TS, t1), (VS, vb3)] if kp == 0 else [(TS, t1)])
                    s2 = SC(lambda e, sB=sB: e.activation(E1B[:, :], sB[:, :], AF.Exp,
                                                          bias=ebias[:, 1:2], scale=1.0),
                            waits=[(TS, t2)])
                    for h01, (sse, e1) in enumerate([(s1, E1A), (s2, E1B)]):
                        g = 2 * h01 + (kp % 2)
                        mt = maskE[:, (kp // 2) * 32:(kp // 2) * 32 + 32]
                        for qb in range(2):
                            T(lambda e, g=g, mt=mt, e1=e1, qb=qb, kp=kp: e.matmul(
                                psD[32 * g:32 * (g + 1), qb * 512:(qb + 1) * 512], mt,
                                e1[:, qb * 512:(qb + 1) * 512],
                                start=(kp < 2), stop=(kp >= NKP - 2),
                                tile_position=(0, 32 * g), skip_group_check=True),
                                waits=[(SS, sse)] if qb == 0 else [])
                T_E = p.now("TS")
                sLn = SC(lambda e: e.activation(LNE[:, :], psD[:, :], AF.Ln, bias=0.0, scale=1.0),
                         waits=[(TS, T_E)])
                V(lambda e, q0=q0: e.tensor_scalar(out=bhi[:, q0:q0 + 1024], in0=LNE[:, :],
                                                   scalar1=-1.0, scalar2=None,
                                                   op0=ALU.mult, op1=ALU.bypass),
                  waits=[(SS, sLn)])
            V_BHI = p.now("VS")

            for qh in range(2):
                q0 = qh * 1024
                prev_s2 = {}
                av_t = None
                first = True
                for kp in range(NKP):
                    for h01 in range(2):
                        reg_i = 2 * h01 + (kp % 2)
                        reg = QHAT[reg_i]
                        g = 2 * h01 + (kp % 2)
                        db = p.dma("sync", reg[64:66, :],
                                   bhi[32 * g + 2 * (kp // 2):32 * g + 2 * (kp // 2) + 2, :],
                                   waits=[(VS, V_BHI), (TS, prev_s2.get(reg_i, 0))], inc=LD)
                        slp = PSLOT[(2 * kp + h01) % 3]
                        T(lambda e, kp=kp, reg=reg, slp=slp, q0=q0: e.matmul(
                            slp[:, 0:512], khat[:, kp * P:(kp + 1) * P], reg[:, q0:q0 + 512],
                            start=True, stop=True, skip_group_check=True),
                            waits=[(LD, db)] + ([(VS, p.now("VS"))] if first else []))
                        t1 = T(lambda e, kp=kp, reg=reg, slp=slp, q0=q0: e.matmul(
                            slp[:, 512:1024], khat[:, kp * P:(kp + 1) * P],
                            reg[:, q0 + 512:q0 + 1024],
                            start=True, stop=True, skip_group_check=True))
                        prev_s2[reg_i] = t1
                        etl = ETL[(2 * kp + h01) % 2]
                        s2e = SC(lambda e, slp=slp, h01=h01, etl=etl: e.activation(
                            etl[:, :], slp[:, :], AF.Exp, bias=ebias[:, 2 + h01:3 + h01],
                            scale=1.0), waits=[(TS, t1)])
                        for qb in range(2):
                            av_t = T(lambda e, kp=kp, h01=h01, qb=qb, etl=etl: e.matmul(
                                psD[64 * h01:64 * (h01 + 1), qb * 512:(qb + 1) * 512],
                                vkt[:, kp * 64:(kp + 1) * 64],
                                etl[:, qb * 512:(qb + 1) * 512],
                                start=(kp == 0), stop=(kp == NKP - 1),
                                tile_position=(0, 64 * h01), skip_group_check=True),
                                waits=[(SS, s2e)] + ([(VS, p.now("VS"))] if first else []))
                        first = False
                va = V(lambda e: e.tensor_copy(out=ASTG[:, :], in_=psD[:, :]),
                       waits=[(TS, av_t), (LD, p.now("LD"))])
                p.dma("sync", aspill.ap()[(b * NPAIR + pr) * P:(b * NPAIR + pr + 1) * P,
                                          q0:q0 + 1024],
                      ASTG[:, :], waits=[(VS, va)], inc=LD)
        LD_ATTN = p.now("LD")

        for r in range(8):
            l = p.dma("sync", FSTG[:, :], io["wo"][r * P:(r + 1) * P, :],
                      waits=([(LD, LD_ATTN)] if r == 0 else []) + [(VS, p.now("VS"))], inc=LD)
            vv = V(lambda e: e.tensor_copy(out=QITM[:, :], in_=FSTG[:, :]),
                   waits=[(LD, l), (VS, p.now("VS"))])
            tt = None
            for fc in range(8):
                tt = T(lambda e, fc=fc: e.transpose(psTb[:, fc * P:(fc + 1) * P],
                                                    QITM[:, fc * P:(fc + 1) * P], identb[:, :]),
                       waits=[(VS, vv), (VS, p.now("VS"))] if fc == 0 else [])
            V(lambda e, r=r: e.tensor_copy(
                out=WTv()[:, :, r * P:(r + 1) * P],
                in_=psTb[:, 0:1024].rearrange("p (c n) -> p c n", c=8, n=P)),
                waits=[(TS, tt)])
        V_WO = p.now("VS")

        def obank(oc):
            return [psA, psB, psC, psD][oc // 2][:, (oc % 2) * 512:(oc % 2 + 1) * 512]

        for qb in range(4):
            lds = None
            for pr in range(NPAIR):
                lds = p.dma("sync", OSTG[:, pr * 512:(pr + 1) * 512],
                            aspill.ap()[(b * NPAIR + pr) * P:(b * NPAIR + pr + 1) * P,
                                        qb * 512:(qb + 1) * 512],
                            waits=[(VS, V_WO)] if (qb == 0 and pr == 0) else [], inc=LD)
            tt = None
            for pr in range(NPAIR):
                for oc in range(8):
                    tt = T(lambda e, pr=pr, oc=oc: e.matmul(
                        obank(oc), WTv()[:, pr, oc * P:(oc + 1) * P],
                        OSTG[:, pr * 512:(pr + 1) * 512],
                        start=(pr == 0), stop=(pr == NPAIR - 1), skip_group_check=True),
                        waits=([(LD, lds), (VS, V_WO), (VS, p.now("VS"))]
                               if (pr == 0 and oc == 0) else []))
            for oc in [7, 6, 0, 1, 2, 3, 4, 5]:
                vv = V(lambda e, oc=oc: e.tensor_copy(out=ASTG[:, 0:512], in_=obank(oc)),
                       waits=[(TS, tt)] if oc == 7 else [(TS, p.now("TS"))])
                t2 = None
                for blk in range(4):
                    t2 = T(lambda e, blk=blk: e.transpose(
                        psTb[:, 1024 + blk * P:1024 + (blk + 1) * P],
                        ASTG[:, blk * P:(blk + 1) * P], identb[:, :]),
                        waits=[(VS, vv)] if blk == 0 else [])
                for blk in range(4):
                    t = TPB * b + qb * 4 + blk
                    x_t = xt(t)
                    V(lambda e, oc=oc, blk=blk, x_t=x_t: e.tensor_tensor(
                        out=x_t[:, oc * P:(oc + 1) * P], in0=x_t[:, oc * P:(oc + 1) * P],
                        in1=psTb[:, 1024 + blk * P:1024 + (blk + 1) * P], op=ALU.add),
                        waits=[(TS, t2)] if blk == 0 else [])

    if "f1" in stages:
        ffn_pass(0)
    if "a0" in stages:
        attention(0)
    if "a1" in stages:
        attention(1)
    if "f2" in stages:
        ffn_pass(1)
    V_END = p.now("VS")
    if debug:
        dbgq = nc.dram_tensor("dbgq", [P, 2048], F32, kind="ExternalOutput").ap()
        p.dma("sync", dbgq[:, 0:1024], UF32[:, 3072:4096], waits=[(VS, V_END)], inc=LD)
        dbq2 = V(lambda e: e.tensor_copy(out=UF32[:, 5120:6144], in_=QI8))
        p.dma("sync", dbgq[:, 1024:2048], UF32[:, 5120:6144], waits=[(VS, dbq2)], inc=LD)
        dbg = nc.dram_tensor("dbg", [P, 192], F32, kind="ExternalOutput").ap()
        p.dma("sync", dbg[:, 0:32], sc[:, :], waits=[(VS, V_END)], inc=LD)
        p.dma("sync", dbg[0:1, 32:64], sc1[:, :], inc=LD)
        p.dma("sync", dbg[:, 64:96], statm[:, :], inc=LD)
        p.dma("sync", dbg[:, 96:128], statr[:, :], inc=LD)
        p.dma("sync", dbg[:, 128:160], stata[:, :], inc=LD)
        p.dma("sync", dbg[:, 160:192], statz[:, :], inc=LD)
    for t in range(NT):
        p.dma("sync", yio[t], xt(t), waits=[(VS, V_END)] if t == 0 else [], inc=LD)
    LD_END = p.now("LD")
    p.emit("sync", lambda e: e.nop(), waits=[(LD, LD_END)])

    p.build(stack)
    stack.close()
    return nc


_NC_CACHE = None


def kernel(L0_attn_hidden_states, L0_attn_w_q, L0_attn_w_k, L0_attn_w_v, L0_attn_w_o,
           L0_attn_w_kgen, L0_attn_b_kgen, L0_attn_w_conv, L0_attn_b_conv,
           L0_ffn_weight, L0_ffn_bias):
    global _NC_CACHE
    if _NC_CACHE is None:
        _NC_CACHE = build_nc()
    nc = _NC_CACHE
    x = np.ascontiguousarray(L0_attn_hidden_states, dtype=np.float32)
    shared = {
        "wq": np.ascontiguousarray(L0_attn_w_q, dtype=np.float32),
        "wk": np.ascontiguousarray(L0_attn_w_k, dtype=np.float32),
        "wv": np.ascontiguousarray(L0_attn_w_v, dtype=np.float32),
        "wo": np.ascontiguousarray(L0_attn_w_o, dtype=np.float32),
        "wkgen": np.ascontiguousarray(L0_attn_w_kgen, dtype=np.float32),
        "bkgen": np.ascontiguousarray(L0_attn_b_kgen, dtype=np.float32).reshape(1, 3),
        "wconv": np.ascontiguousarray(L0_attn_w_conv, dtype=np.float32).reshape(HID, 3),
        "bconv": np.ascontiguousarray(L0_attn_b_conv, dtype=np.float32).reshape(1, HID),
        "wffn": np.ascontiguousarray(L0_ffn_weight, dtype=np.float32),
        "bffn": np.ascontiguousarray(L0_ffn_bias, dtype=np.float32).reshape(1, HID),
    }
    in_maps = []
    for c in range(8):
        m = dict(shared)
        m["x"] = np.ascontiguousarray(x[2 * c:2 * c + 2].reshape(B * S, HID))
        in_maps.append(m)
    res = run_bass_kernel_spmd(nc, in_maps, core_ids=list(range(8)))
    out = np.concatenate([np.asarray(r["y"]).reshape(B, S, HID) for r in res.results], axis=0)
    return out.astype(np.float32)


# version 06
# speedup vs baseline: 1.2103x; 1.2103x over previous
import math
import contextlib
import numpy as np
import ml_dtypes
import concourse.bass as bass
import concourse.mybir as mybir
from concourse.bass_utils import run_bass_kernel_spmd

F32 = mybir.dt.float32
BF16 = mybir.dt.bfloat16
I8 = mybir.dt.int8
AF = mybir.ActivationFunctionType
ALU = mybir.AluOpType
AX = mybir.AxisListType

P = 128
B = 2
S = 2048
HID = 1024
NT = 32
TPB = 16
NH = 16
HD = 64
NPAIR = 8
NKP = 16
LN32 = math.log(32.0 + 1e-9)

ENGINES = ["sync", "scalar", "vector", "tensor", "gpsimd"]


class Prog:
    def __init__(self, nc):
        self.nc = nc
        self.ops = {e: [] for e in ENGINES}
        self.counts = {}
        self.handles = {}
        self.owner = {}

    def sem(self, name, owner):
        self.counts[name] = 0
        self.owner[name] = owner
        return name

    def now(self, name):
        return self.counts[name]

    def emit(self, eng, fn, waits=(), inc=None, inc_n=1):
        if inc is not None:
            assert self.owner[inc] == eng, (inc, eng)
            self.counts[inc] += inc_n
        self.ops[eng].append((fn, tuple(waits), inc, inc_n))
        return self.counts[inc] if inc is not None else None

    def dma(self, eng, dst, src, waits=(), inc=None, slow=False):
        return self.emit(eng, (lambda d, s, sl: (lambda e: e.dma_start(
            out=d, in_=s, allow_slow_non_contiguous=sl)))(dst, src, slow),
            waits=waits, inc=inc, inc_n=16)

    def build(self, stack):
        nc = self.nc
        for name in self.counts:
            self.handles[name] = stack.enter_context(nc.semaphore(name))
        with nc.Block() as block:
            def mk(e):
                def run(eng):
                    for fn, waits, inc, inc_n in self.ops[e]:
                        for s, v in waits:
                            if v > 0:
                                eng.wait_ge(self.handles[s], v)
                        ins = fn(eng)
                        if inc is not None:
                            ins.then_inc(self.handles[inc], inc_n)
                return run
            for e in ENGINES:
                if self.ops[e]:
                    getattr(block, e)(mk(e))


def build_nc(stages=("f1", "a0", "a1", "f2"), debug=False):
    nc = bass.Bass("TRN2", target_bir_lowering=False)

    io = {}
    io["x"] = nc.dram_tensor("x", [B * S, HID], F32, kind="ExternalInput").ap()
    io["wq"] = nc.dram_tensor("wq", [HID, HID], F32, kind="ExternalInput").ap()
    io["wk"] = nc.dram_tensor("wk", [256, HID], F32, kind="ExternalInput").ap()
    io["wv"] = nc.dram_tensor("wv", [256, HID], F32, kind="ExternalInput").ap()
    io["wo"] = nc.dram_tensor("wo", [HID, HID], F32, kind="ExternalInput").ap()
    io["wkgen"] = nc.dram_tensor("wkgen", [3, HID], F32, kind="ExternalInput").ap()
    io["bkgen"] = nc.dram_tensor("bkgen", [1, 3], F32, kind="ExternalInput").ap()
    io["wconv"] = nc.dram_tensor("wconv", [HID, 3], F32, kind="ExternalInput").ap()
    io["bconv"] = nc.dram_tensor("bconv", [1, HID], F32, kind="ExternalInput").ap()
    io["wffn"] = nc.dram_tensor("wffn", [HID, HID], F32, kind="ExternalInput").ap()
    io["bffn"] = nc.dram_tensor("bffn", [1, HID], F32, kind="ExternalInput").ap()
    io["y"] = nc.dram_tensor("y", [B * S, HID], F32, kind="ExternalOutput").ap()

    cc_in = [nc.dram_tensor(f"cc_in{i}", [1, 8], F32) for i in range(2)]
    cc_out = [nc.dram_tensor(f"cc_out{i}", [1, 8], F32) for i in range(2)]
    aspill = nc.dram_tensor("aspill", [B * NPAIR * P, S], BF16)

    p = Prog(nc)
    LD = p.sem("LD", "sync")
    GD = p.sem("GD", "gpsimd")
    TS = p.sem("TS", "tensor")
    VS = p.sem("VS", "vector")
    SS = p.sem("SS", "scalar")

    stack = contextlib.ExitStack()
    sb = lambda name, shape, dt: stack.enter_context(nc.sbuf_tensor(name, shape, dt))
    ps = lambda name, shape, dt: stack.enter_context(nc.psum_tensor(name, shape, dt))

    x_sb = sb("x_sb", [P, NT * HID], F32)
    ufm = sb("ufm", [P, 8 * S], BF16)
    pool = sb("pool", [P, 26624], mybir.dt.uint8)
    bhi = sb("bhi", [P, S], BF16)
    kdup = sb("kdup", [P, S], BF16)
    khat = sb("khat", [66, S], BF16)
    vkt = sb("vkt", [P, NKP * HD], BF16)
    qpair = sb("qpair", [P, S], BF16)
    maskE = sb("maskE", [P, 8 * 32], BF16)
    statm = sb("statm", [P, NT], F32)
    statr = sb("statr", [P, NT], F32)
    stata = sb("stata", [P, NT], F32)
    statb = sb("statb", [P, NT], F32)
    statz = sb("statz", [P, NT], F32)
    ident = sb("ident", [P, P], F32)
    identb = sb("identb", [P, P], BF16)
    ones_r = sb("ones_r", [1, P], F32)
    ones_rb = sb("ones_rb", [1, P], BF16)
    ind64 = sb("ind64", [64, 1], BF16)
    indAB = sb("indAB", [P, 2], BF16)
    sc = sb("sc", [P, 32], F32)
    sc1 = sb("sc1", [1, 32], F32)
    stage = None
    convw = sb("convw", [P, 8 * 3], F32)
    convb = sb("convb", [P, 8], F32)
    ctxa = sb("ctxa", [P, 8], F32)
    ctxb = sb("ctxb", [P, 8], BF16)
    wkg = sb("wkg", [P, 8 * 3], BF16)
    gvec = sb("gvec", [P, 2], F32)
    ebias = sb("ebias", [P, 4], F32)
    ln32t = sb("ln32t", [P, 1], F32)
    padt = sb("padt", [P, 64], F32)

    vfmt = bhi[0:64, :]
    UF32 = ufm.bitcast(F32)
    UB16 = ufm
    BROW = UF32[0:1, 4096:5120]
    BROWB = UB16[0:1, 10240:11264]

    poolb = pool.bitcast(BF16)
    poolf = pool.bitcast(F32)
    tmpv = poolf[:, 4096:4128]
    tmpv2 = poolf[:, 4128:4160]

    WT = poolb[:, 0:8192]
    WTv = lambda: WT.rearrange("p (c n) -> p c n", c=8, n=1024)
    FSTG = poolf[:, 4096:5120]
    QITM = poolb[:, 10240:11264]
    QIFM = poolb[:, 11264:12288]
    QI8 = pool.bitcast(I8)[:, 24576:25600]
    Z2R = [poolb[:, 0:4112], poolb[:, 4112:8224]]
    z2v = lambda i: Z2R[i].rearrange("p (c n) -> p c n", c=8, n=514)
    Z2TM = poolf[:, 4112 // 2 + 2056:0] if False else poolf[:, 4112:5136]
    CVT0 = poolf[:, 5136:5648]
    CVT1 = poolf[:, 5648:6160]
    QHAT = [poolb[0:66, 0:2048], poolb[0:66, 2048:4096],
            poolb[0:66, 4096:6144], poolb[0:66, 6144:8192]]
    E1A = poolb[:, 8192:9216]
    E1B = poolb[:, 9216:10240]
    ETL = [poolb[:, 10240:11264], poolb[:, 11264:12288]]
    ASTG = poolb[:, 12288:13312]
    wslice = ASTG
    LNE = poolf[:, 4096:5120]
    QSQ = poolb[:, 8192:10240]
    OSTG = poolb[:, 8192:12288]
    ufmv = lambda: ufm.rearrange("p (c n) -> p c n", c=8, n=S)
    convwv = convw.rearrange("p (c k) -> p c k", c=8, k=3)

    psA = ps("psA", [P, 1024], F32)
    psB = ps("psB", [P, 1024], F32)
    psC = ps("psC", [P, 1024], F32)
    psD = ps("psD", [P, 1024], F32)
    PSLOT = [psA, psB, psC]
    psTb = psD.bitcast(BF16)
    psCb = psC.bitcast(BF16)

    def T(fn, waits=()):
        return p.emit("tensor", fn, waits=waits, inc=TS)

    def V(fn, waits=()):
        return p.emit("vector", fn, waits=waits, inc=VS)

    def SC(fn, waits=()):
        return p.emit("scalar", fn, waits=waits, inc=SS)

    def G(fn, waits=()):
        return p.emit("gpsimd", fn, waits=waits, inc=GD, inc_n=1)

    def pad():
        V(lambda e: e.memset(padt[:, :], 0.0))
        return V(lambda e: e.memset(padt[:, :], 1.0))

    def bcast128(dst_col, src11, wait_v, psum_slice, wait_s=0):
        t = T(lambda e, s=src11, o=psum_slice: e.matmul(
            o, ones_r[0:1, :], s, start=True, stop=True, skip_group_check=True),
            waits=[(VS, wait_v), (SS, wait_s)])
        V(lambda e, o=psum_slice, d=dst_col: e.tensor_copy(out=d, in_=o),
          waits=[(TS, t)])
        return pad()

    xt = lambda t: x_sb[:, t * HID:(t + 1) * HID]
    xio = io["x"].rearrange("(n p) h -> n p h", p=P)
    yio = io["y"].rearrange("(n p) h -> n p h", p=P)
    for t in range(NT):
        p.dma("sync", xt(t), xio[t], inc=LD)
    LD_X = p.now("LD")

    p.dma("sync", convw.rearrange("p (c k) -> p c k", c=8, k=3),
          io["wconv"].rearrange("(c p) k -> p c k", p=P), inc=LD)
    p.dma("sync", convb.rearrange("p (c o) -> p c o", c=8, o=1),
          io["bconv"].rearrange("o (c p) -> p c o", p=P), inc=LD, slow=True)
    p.dma("sync", FSTG[0:3, 0:1024], io["wkgen"], inc=LD)
    LD_WKG = p.now("LD")

    G(lambda e: e.memset(ident[:, :], 1.0))
    g_id = G(lambda e: e.affine_select(ident[:, :], ident[:, :], [[1, P]], ALU.is_equal, 0.0,
                                       base=0, channel_multiplier=-1))
    v_idb = V(lambda e: e.tensor_copy(out=identb[:, :], in_=ident[:, :]), waits=[(GD, g_id)])
    v_wkgc = V(lambda e: e.tensor_copy(out=QITM[0:3, 0:1024], in_=FSTG[0:3, 0:1024]),
               waits=[(LD, LD_WKG)])
    t_wkg = None
    for c in range(8):
        t_wkg = T(lambda e, c=c: e.transpose(psTb[:, c * 4:c * 4 + 3],
                                             QITM[0:3, c * P:(c + 1) * P], identb[0:3, 0:3]),
                  waits=[(VS, max(v_idb, v_wkgc))] if c == 0 else [])
    v_wkg = V(lambda e: e.tensor_copy(
        out=wkg.rearrange("p (c k) -> p c k", c=8, k=3),
        in_=psTb[:, 0:32].rearrange("p (c k) -> p c k", c=8, k=4)[:, :, 0:3]),
        waits=[(TS, t_wkg)])
    G(lambda e: e.memset(ones_r[:, :], 1.0))
    G(lambda e: e.memset(ones_rb[:, :], 1.0))
    G(lambda e: e.memset(ind64[:, :], 1.0))
    G(lambda e: e.memset(indAB[:, :], 0.0))
    G(lambda e: e.memset(indAB[0:64, 0:1], 1.0))
    G(lambda e: e.memset(indAB[64:128, 1:2], 1.0))
    G(lambda e: e.memset(ln32t[:, :], LN32))
    G(lambda e: e.memset(maskE[:, :], 0.0))
    for j in range(8):
        G(lambda e, j=j: e.memset(maskE[0:64, j * 32 + 2 * j: j * 32 + 2 * j + 1], 1.0))
        G(lambda e, j=j: e.memset(maskE[64:128, j * 32 + 2 * j + 1: j * 32 + 2 * j + 2], 1.0))
    G_CONST = p.now("GD")

    def ffn_pass(idx):
        v_i = V(lambda e: e.memset(sc[:, 0:1], 0.0))
        for r in range(8):
            l = p.dma("sync", FSTG[:, :], io["wffn"][r * P:(r + 1) * P, :],
                      waits=([(VS, v_wkg)] if (idx == 0 and r == 0) else
                             ([(VS, p.now("VS"))] if r == 0 else [])) +
                            [(SS, p.now("SS"))], inc=LD)
            s_abs = SC(lambda e: e.activation(UF32[:, 0:1024], FSTG[:, :], AF.Abs,
                                              bias=0.0, scale=1.0),
                       waits=[(LD, l), (VS, p.now("VS"))])
            v_r = V(lambda e: e.tensor_reduce(out=sc[:, 1:2], in_=UF32[:, 0:1024], axis=AX.X,
                                              op=ALU.add), waits=[(SS, s_abs)])
            pad()
            v_i = V(lambda e: e.tensor_tensor(out=sc[:, 0:1], in0=sc[:, 0:1], in1=sc[:, 1:2],
                                              op=ALU.add))
        t_tr = T(lambda e: e.transpose(psC[0:1, 0:P], sc[:, 0:1], ident[:, :]),
                 waits=[(VS, v_i), (GD, G_CONST)])
        v_s = V(lambda e: e.tensor_reduce(out=sc1[0:1, 0:1], in_=psC[0:1, 0:P], axis=AX.X,
                                          op=ALU.add), waits=[(TS, t_tr)])
        pad()
        v_thr = V(lambda e: e.tensor_scalar(out=sc1[0:1, 1:2], in0=sc1[0:1, 0:1],
                                            scalar1=0.5 / (1024.0 * 1024.0), scalar2=None,
                                            op0=ALU.mult, op1=ALU.bypass))
        v_ws = V(lambda e: e.tensor_scalar(out=sc1[0:1, 2:3], in0=sc1[0:1, 0:1],
                                           scalar1=1.0 / (1024.0 * 1024.0), scalar2=None,
                                           op0=ALU.mult, op1=ALU.bypass))
        v_thrb = bcast128(sc[:, 2:3], sc1[0:1, 1:2], v_thr, psC[:, 512:513])
        v_thrn = V(lambda e: e.tensor_scalar(out=sc[:, 3:4], in0=sc[:, 2:3], scalar1=-1.0,
                                             scalar2=None, op0=ALU.mult, op1=ALU.bypass))
        for r in range(8):
            l = p.dma("sync", FSTG[:, :], io["wffn"][r * P:(r + 1) * P, :],
                      waits=([(SS, p.now("SS"))] if r == 0 else []) + [(VS, p.now("VS"))],
                      inc=LD)
            v1 = V(lambda e: e.tensor_scalar(out=UF32[:, 0:1024], in0=FSTG[:, :],
                                             scalar1=sc[:, 2:3], scalar2=None,
                                             op0=ALU.is_gt, op1=ALU.bypass),
                   waits=[(LD, l), (VS, v_thrn)])
            v2 = V(lambda e: e.tensor_scalar(out=UF32[:, 1024:2048], in0=FSTG[:, :],
                                             scalar1=sc[:, 3:4], scalar2=None,
                                             op0=ALU.is_lt, op1=ALU.bypass))
            v3 = V(lambda e: e.tensor_tensor(out=QITM[:, :], in0=UF32[:, 0:1024],
                                             in1=UF32[:, 1024:2048], op=ALU.subtract))
            tt = None
            for fc in range(8):
                tt = T(lambda e, fc=fc: e.transpose(psTb[:, fc * P:(fc + 1) * P],
                                                    QITM[:, fc * P:(fc + 1) * P], identb[:, :]),
                       waits=[(VS, max(v3, v_idb)), (VS, p.now("VS"))] if fc == 0 else [])
            V(lambda e, r=r: e.tensor_copy(
                out=WTv()[:, :, r * P:(r + 1) * P],
                in_=psTb[:, 0:1024].rearrange("p (c n) -> p c n", c=8, n=P)),
                waits=[(TS, tt)])
        for t in range(NT):
            x_t = xt(t)
            va = V(lambda e, x_t=x_t: e.tensor_tensor(out=UF32[:, 2048:3072], in0=x_t, in1=x_t,
                                                      op=ALU.mult),
                   waits=[(LD, LD_X)] if t == 0 else [])
            V(lambda e, t=t, x_t=x_t: e.tensor_reduce(out=statm[:, t:t + 1], in_=x_t,
                                                      axis=AX.X, op=ALU.add))
            V(lambda e, t=t: e.tensor_reduce(out=statr[:, t:t + 1], in_=UF32[:, 2048:3072],
                                             axis=AX.X, op=ALU.add))
            V(lambda e, t=t, x_t=x_t: e.tensor_reduce(out=tmpv[:, t:t + 1], in_=x_t,
                                                      axis=AX.X, op=ALU.max))
            V(lambda e, t=t, x_t=x_t: e.tensor_reduce(out=tmpv2[:, t:t + 1], in_=x_t,
                                                      axis=AX.X, op=ALU.min))
        V(lambda e: e.tensor_scalar(out=statm[:, :], in0=statm[:, :], scalar1=1.0 / HID,
                                    scalar2=None, op0=ALU.mult, op1=ALU.bypass))
        V(lambda e: e.tensor_scalar(out=statr[:, :], in0=statr[:, :], scalar1=1.0 / HID,
                                    scalar2=None, op0=ALU.mult, op1=ALU.bypass))
        pad()
        V(lambda e: e.tensor_tensor(out=statz[:, :], in0=statm[:, :], in1=statm[:, :],
                                    op=ALU.mult))
        pad()
        V(lambda e: e.tensor_tensor(out=statr[:, :], in0=statr[:, :], in1=statz[:, :],
                                    op=ALU.subtract))
        pad()
        V(lambda e: e.tensor_scalar(out=statr[:, :], in0=statr[:, :], scalar1=1.0,
                                    scalar2=1e-5, op0=ALU.mult, op1=ALU.add))
        pad()
        vr0 = V(lambda e: e.reciprocal(out=statz[:, :], in_=statr[:, :]))
        vr0 = pad()
        ss0 = SC(lambda e: e.activation(stata[:, :], statz[:, :], AF.Sqrt, bias=0.0, scale=1.0),
                 waits=[(VS, vr0)])
        V(lambda e: e.tensor_tensor(out=statb[:, :], in0=stata[:, :], in1=stata[:, :],
                                    op=ALU.mult), waits=[(SS, ss0)])
        pad()
        V(lambda e: e.tensor_tensor(out=statb[:, :], in0=statb[:, :], in1=statr[:, :],
                                    op=ALU.mult))
        pad()
        V(lambda e: e.tensor_scalar(out=statb[:, :], in0=statb[:, :], scalar1=-0.5,
                                    scalar2=1.5, op0=ALU.mult, op1=ALU.add))
        pad()
        V(lambda e: e.tensor_tensor(out=statr[:, :], in0=stata[:, :], in1=statb[:, :],
                                    op=ALU.mult))
        pad()
        V(lambda e: e.tensor_tensor(out=tmpv[:, :], in0=tmpv[:, :], in1=statm[:, :],
                                    op=ALU.subtract))
        V(lambda e: e.tensor_tensor(out=tmpv2[:, :], in0=statm[:, :], in1=tmpv2[:, :],
                                    op=ALU.subtract))
        pad()
        V(lambda e: e.tensor_tensor(out=tmpv[:, :], in0=tmpv[:, :], in1=tmpv2[:, :],
                                    op=ALU.max))
        pad()
        V(lambda e: e.tensor_tensor(out=statz[:, :], in0=tmpv[:, :], in1=statr[:, :],
                                    op=ALU.mult))
        pad()
        vz5 = V(lambda e: e.tensor_reduce(out=sc[:, 4:5], in_=statz[:, :], axis=AX.X, op=ALU.max))
        tz = T(lambda e: e.transpose(psC[0:1, 0:P], sc[:, 4:5], ident[:, :]), waits=[(VS, vz5)])
        vz6 = V(lambda e: e.tensor_reduce(out=sc1[0:1, 4:5], in_=psC[0:1, 0:P], axis=AX.X,
                                          op=ALU.max), waits=[(TS, tz)])
        vz6 = pad()
        ts8 = T(lambda e: e.matmul(psC[0:1, 512:520], sc1[0:1, 4:5], ones_r[0:1, 0:8],
                                   start=True, stop=True, skip_group_check=True),
                waits=[(VS, vz6)])
        vs8 = V(lambda e: e.tensor_copy(out=sc1[0:1, 16:24], in_=psC[0:1, 512:520]), waits=[(TS, ts8)])
        g1 = p.dma("gpsimd", cc_in[idx].ap(), sc1[0:1, 16:24], waits=[(VS, vs8)], inc=GD)
        g2 = G(lambda e: e.collective_compute(
            "AllReduce", mybir.AluOpType.max, replica_groups=[list(range(8))],
            ins=[cc_in[idx].ap()], outs=[cc_out[idx].ap()]), waits=[(GD, g1)])
        g3 = p.dma("gpsimd", sc1[0:1, 5:6], cc_out[idx].ap()[0:1, 0:1], waits=[(GD, g2)], inc=GD)
        V(lambda e: e.tensor_scalar(out=sc1[0:1, 6:7], in0=sc1[0:1, 5:6], scalar1=1.0 / 127.0,
                                    scalar2=None, op0=ALU.mult, op1=ALU.bypass),
          waits=[(GD, g3)])
        pad()
        vi2 = V(lambda e: e.reciprocal(out=sc1[0:1, 7:8], in_=sc1[0:1, 6:7]))
        pad()
        vi3 = V(lambda e: e.tensor_tensor(out=sc1[0:1, 8:9], in0=sc1[0:1, 2:3],
                                          in1=sc1[0:1, 6:7], op=ALU.mult))
        pad()
        vi4 = V(lambda e: e.tensor_scalar(out=sc1[0:1, 9:10], in0=sc1[0:1, 8:9], scalar1=0.5,
                                          scalar2=None, op0=ALU.mult, op1=ALU.bypass))
        vi5 = V(lambda e: e.reciprocal(out=sc1[0:1, 10:11], in_=sc1[0:1, 8:9]))
        vi5 = pad()
        lb = p.dma("sync", BROW, io["bffn"], waits=[(SS, p.now("SS"))], inc=LD)
        vi6 = V(lambda e: e.tensor_scalar(out=BROWB, in0=BROW, scalar1=sc1[0:1, 10:11],
                                          scalar2=None, op0=ALU.mult, op1=ALU.bypass),
                waits=[(LD, lb), (VS, vi5)])
        v_ri = bcast128(sc[:, 5:6], sc1[0:1, 7:8], vi2, psC[:, 520:521])
        v_w2 = bcast128(sc[:, 6:7], sc1[0:1, 9:10], vi4, psC[:, 521:522])
        V(lambda e: e.tensor_scalar(out=stata[:, :], in0=statr[:, :], scalar1=sc[:, 5:6],
                                    scalar2=None, op0=ALU.mult, op1=ALU.bypass),
          waits=[(VS, v_ri)])
        pad()
        V(lambda e: e.tensor_tensor(out=statb[:, :], in0=statm[:, :], in1=stata[:, :],
                                    op=ALU.mult))
        pad()
        V(lambda e: e.tensor_scalar(out=statb[:, :], in0=statb[:, :], scalar1=-1.0,
                                    scalar2=None, op0=ALU.mult, op1=ALU.bypass))
        va3 = pad()
        for t in range(NT):
            x_t = xt(t)
            out_ps = psA if t % 2 == 0 else psB
            vq1 = V(lambda e, t=t, x_t=x_t: e.tensor_scalar(
                out=QI8, in0=x_t, scalar1=stata[:, t:t + 1], scalar2=statb[:, t:t + 1],
                op0=ALU.mult, op1=ALU.add), waits=[(VS, va3)])
            vq2 = V(lambda e: e.tensor_copy(out=QITM[:, :], in_=QI8))
            tt = None
            for fc in range(8):
                tt = T(lambda e, fc=fc: e.transpose(psTb[:, 1024 + fc * P:1024 + (fc + 1) * P],
                                                    QITM[:, fc * P:(fc + 1) * P], identb[:, :]),
                       waits=[(VS, vq2)] if fc == 0 else [])
            vq3 = V(lambda e: e.tensor_copy(out=QIFM[:, :], in_=psTb[:, 1024:2048]),
                    waits=[(TS, tt)])
            tm = T(lambda e, o=out_ps: e.matmul(o[:, 0:512], ones_rb[0:1, :], BROWB[0:1, 0:512],
                                                start=True, stop=False, skip_group_check=True),
                   waits=[(VS, max(vq3, vi6))])
            tm = T(lambda e, o=out_ps: e.matmul(o[:, 512:1024], ones_rb[0:1, :],
                                                BROWB[0:1, 512:1024],
                                                start=True, stop=False, skip_group_check=True))
            for fc in range(8):
                for half in range(2):
                    tm = T(lambda e, fc=fc, half=half, o=out_ps: e.matmul(
                        o[:, half * 512:(half + 1) * 512],
                        QIFM[:, fc * P:(fc + 1) * P],
                        WTv()[:, fc, half * 512:(half + 1) * 512],
                        start=False, stop=(fc == 7), skip_group_check=True))
            V(lambda e, o=out_ps: e.tensor_scalar(
                out=UF32[:, 3072:4096], in0=o[:, :], scalar1=sc[:, 6:7], scalar2=None,
                op0=ALU.mult, op1=ALU.bypass), waits=[(TS, tm), (VS, v_w2)])
            V(lambda e, x_t=x_t: e.tensor_tensor(out=x_t, in0=x_t, in1=UF32[:, 3072:4096],
                                                 op=ALU.add))

    def attention(b):
        sl = slice(TPB * b, TPB * (b + 1))
        for i, t in enumerate(range(TPB * b, TPB * (b + 1))):
            x_t = xt(t)
            V(lambda e, x_t=x_t: e.tensor_tensor(out=Z2TM[:, :], in0=x_t, in1=x_t, op=ALU.mult),
              waits=[(VS, p.now("VS"))] if i == 0 else [])
            V(lambda e, t=t, x_t=x_t: e.tensor_reduce(out=statm[:, t:t + 1], in_=x_t,
                                                      axis=AX.X, op=ALU.add))
            V(lambda e, t=t: e.tensor_reduce(out=statr[:, t:t + 1], in_=Z2TM[:, :],
                                             axis=AX.X, op=ALU.add))
        V(lambda e: e.tensor_scalar(out=statm[:, sl], in0=statm[:, sl], scalar1=1.0 / HID,
                                    scalar2=None, op0=ALU.mult, op1=ALU.bypass))
        V(lambda e: e.tensor_scalar(out=statr[:, sl], in0=statr[:, sl], scalar1=1.0 / HID,
                                    scalar2=None, op0=ALU.mult, op1=ALU.bypass))
        pad()
        V(lambda e: e.tensor_tensor(out=statz[:, sl], in0=statm[:, sl], in1=statm[:, sl],
                                    op=ALU.mult))
        pad()
        V(lambda e: e.tensor_tensor(out=statr[:, sl], in0=statr[:, sl], in1=statz[:, sl],
                                    op=ALU.subtract))
        pad()
        V(lambda e: e.tensor_scalar(out=statr[:, sl], in0=statr[:, sl], scalar1=1.0,
                                    scalar2=1e-5, op0=ALU.mult, op1=ALU.add))
        pad()
        vr0 = V(lambda e: e.reciprocal(out=statz[:, sl], in_=statr[:, sl]))
        vr0 = pad()
        ss0 = SC(lambda e: e.activation(stata[:, sl], statz[:, sl], AF.Sqrt, bias=0.0, scale=1.0),
                 waits=[(VS, vr0)])
        V(lambda e: e.tensor_tensor(out=statb[:, sl], in0=stata[:, sl], in1=stata[:, sl],
                                    op=ALU.mult), waits=[(SS, ss0)])
        pad()
        V(lambda e: e.tensor_tensor(out=statb[:, sl], in0=statb[:, sl], in1=statr[:, sl],
                                    op=ALU.mult))
        pad()
        V(lambda e: e.tensor_scalar(out=statb[:, sl], in0=statb[:, sl], scalar1=-0.5,
                                    scalar2=1.5, op0=ALU.mult, op1=ALU.add))
        pad()
        V(lambda e: e.tensor_tensor(out=statr[:, sl], in0=stata[:, sl], in1=statb[:, sl],
                                    op=ALU.mult))
        pad()
        V(lambda e: e.tensor_tensor(out=statb[:, sl], in0=statm[:, sl], in1=statr[:, sl],
                                    op=ALU.mult))
        pad()
        V(lambda e: e.tensor_scalar(out=statb[:, sl], in0=statb[:, sl], scalar1=-1.0,
                                    scalar2=None, op0=ALU.mult, op1=ALU.bypass))
        vmb2 = pad()
        V(lambda e: e.memset(ctxa[:, :], 0.0))

        def produce_z2(j):
            slot = z2v(j % 2)
            for q in range(4):
                t = TPB * b + 4 * j + q
                x_t = xt(t)
                va = V(lambda e, t=t, x_t=x_t: e.tensor_scalar(
                    out=Z2TM[:, :], in0=x_t, scalar1=statr[:, t:t + 1], scalar2=statb[:, t:t + 1],
                    op0=ALU.mult, op1=ALU.add))
                tt = None
                for fc in range(8):
                    dst = psA if fc < 4 else psB
                    tt = T(lambda e, fc=fc, dst=dst: e.transpose(
                        dst[:, (fc % 4) * P:(fc % 4 + 1) * P], Z2TM[:, fc * P:(fc + 1) * P],
                        ident[:, :]),
                        waits=[(VS, p.now("VS"))] if fc == 0 else [])
                V(lambda e, q=q, slot=slot: e.tensor_copy(
                    out=slot[:, 0:4, 1 + q * P:1 + (q + 1) * P],
                    in_=psA[:, 0:512].rearrange("p (c n) -> p c n", c=4, n=P)), waits=[(TS, tt)])
                V(lambda e, q=q, slot=slot: e.tensor_copy(
                    out=slot[:, 4:8, 1 + q * P:1 + (q + 1) * P],
                    in_=psB[:, 0:512].rearrange("p (c n) -> p c n", c=4, n=P)))
            V(lambda e, slot=slot: e.tensor_reduce(out=sc[:, 16:24], in_=slot[:, :, 1:513],
                                                   axis=AX.X, op=ALU.add))
            pad()
            V(lambda e: e.tensor_tensor(out=ctxa[:, :], in0=ctxa[:, :], in1=sc[:, 16:24],
                                        op=ALU.add))
            pad()
            if j == 0:
                V(lambda e, slot=slot: e.memset(slot[:, :, 0:1], 0.0))
            else:
                prev = z2v((j - 1) % 2)
                V(lambda e, slot=slot, prev=prev: e.tensor_copy(
                    out=slot[:, :, 0:1], in_=prev[:, :, 512:513]))
                V(lambda e, slot=slot, prev=prev: e.tensor_copy(
                    out=prev[:, :, 513:514], in_=slot[:, :, 1:2]))
            if j == 3:
                V(lambda e, slot=slot: e.memset(slot[:, :, 513:514], 0.0))

        def conv_block(j):
            slot = z2v(j % 2)
            for fc in range(8):
                zc = slot[:, fc]
                w0, w1, w2 = (convwv[:, fc, k:k + 1] for k in range(3))
                bcv = convb[:, fc:fc + 1]
                V(lambda e, zc=zc, w1=w1, bcv=bcv: e.tensor_scalar(
                    out=CVT0[:, :], in0=zc[:, 1:513], scalar1=w1, scalar2=bcv,
                    op0=ALU.mult, op1=ALU.add))
                V(lambda e, zc=zc, w0=w0: e.tensor_scalar(
                    out=CVT1[:, :], in0=zc[:, 0:512], scalar1=w0, scalar2=None,
                    op0=ALU.mult, op1=ALU.bypass))
                V(lambda e: e.tensor_tensor(out=CVT0[:, :], in0=CVT0[:, :], in1=CVT1[:, :],
                                            op=ALU.add))
                V(lambda e, zc=zc, w2=w2: e.tensor_scalar(
                    out=CVT1[:, :], in0=zc[:, 2:514], scalar1=w2, scalar2=None,
                    op0=ALU.mult, op1=ALU.bypass))
                V(lambda e, fc=fc, j=j: e.tensor_tensor(
                    out=ufmv()[:, fc, j * 512:(j + 1) * 512],
                    in0=CVT0[:, :], in1=CVT1[:, :], op=ALU.add))

        produce_z2(0)
        for j in range(4):
            if j < 3:
                produce_z2(j + 1)
            conv_block(j)
        V_U = p.now("VS")
        vg1 = V(lambda e: e.tensor_scalar(out=ctxb[:, :], in0=ctxa[:, :], scalar1=1.0 / S,
                                          scalar2=None, op0=ALU.mult, op1=ALU.bypass))
        tg = None
        for c in range(8):
            tg = T(lambda e, c=c: e.matmul(
                psC[0:1, 640:643], ctxb[:, c:c + 1], wkg.rearrange("p (c k) -> p c k", c=8, k=3)[:, c],
                start=(c == 0), stop=(c == 7), skip_group_check=True),
                waits=[(VS, vg1)] if c == 0 else [])
        lg = p.dma("sync", sc1[0:1, 12:15], io["bkgen"], inc=LD)
        vg2 = V(lambda e: e.tensor_copy(out=sc1[0:1, 16:19], in_=psC[0:1, 640:643]),
                waits=[(TS, tg)])
        pad()
        vg3 = V(lambda e: e.tensor_tensor(out=sc1[0:1, 16:19], in0=sc1[0:1, 16:19],
                                          in1=sc1[0:1, 12:15], op=ALU.add), waits=[(LD, lg)])
        sg1 = SC(lambda e: e.activation(sc1[0:1, 20:23], sc1[0:1, 16:19], AF.Sigmoid,
                                        bias=0.0, scale=1.0), waits=[(VS, vg3)])
        vg4 = V(lambda e: e.tensor_reduce(out=sc1[0:1, 24:25], in_=sc1[0:1, 20:23], axis=AX.X,
                                          op=ALU.add), waits=[(SS, sg1)])
        pad()
        vg5 = V(lambda e: e.tensor_scalar(out=sc1[0:1, 25:26], in0=sc1[0:1, 24:25],
                                          scalar1=1.0 / 24.0, scalar2=None, op0=ALU.mult,
                                          op1=ALU.bypass))
        vg6 = V(lambda e: e.tensor_scalar(out=sc1[0:1, 26:27], in0=sc1[0:1, 24:25],
                                          scalar1=1.0 / 3.0, scalar2=None, op0=ALU.mult,
                                          op1=ALU.bypass))
        vg6 = pad()
        v_qs = bcast128(gvec[:, 0:1], sc1[0:1, 25:26], vg5, psC[:, 644:645])
        v_ks = bcast128(gvec[:, 1:2], sc1[0:1, 26:27], vg6, psC[:, 645:646])

        def build_wT(w_io, r0, nrows, extra_waits=()):
            l = p.dma("sync", FSTG[0:nrows, 0:HID], w_io[r0:r0 + nrows, :],
                      waits=[(VS, p.now("VS"))], inc=LD)
            vv = V(lambda e, nrows=nrows: e.tensor_copy(out=QITM[0:nrows, :],
                                                        in_=FSTG[0:nrows, 0:HID]),
                   waits=[(LD, l)] + list(extra_waits))
            tt = None
            for fc in range(8):
                tt = T(lambda e, fc=fc, nrows=nrows: e.transpose(
                    psTb[:, fc * nrows:(fc + 1) * nrows], QITM[0:nrows, fc * P:(fc + 1) * P],
                    identb[0:nrows, 0:nrows]),
                    waits=[(VS, vv), (VS, p.now("VS"))] if fc == 0 else [])
            return V(lambda e, nrows=nrows: e.tensor_copy(out=wslice[:, 0:8 * nrows],
                                                          in_=psTb[:, 0:8 * nrows]),
                     waits=[(TS, tt), (LD, p.now("LD"))])

        def project(nrows, dst_rows, scale_col, evac_to, wait_w):
            tprev = None
            for tb in range(4):
                dstp = psA if tb % 2 == 0 else psB
                half = (tb // 2) * 512
                for fc in range(8):
                    tprev = T(lambda e, tb=tb, fc=fc, dstp=dstp, half=half, nrows=nrows: e.matmul(
                        dstp[0:nrows, half:half + 512],
                        wslice[:, fc * nrows:(fc + 1) * nrows],
                        ufmv()[:, fc, tb * 512:(tb + 1) * 512],
                        start=(fc == 0), stop=(fc == 7), skip_group_check=True),
                        waits=[(VS, wait_w)] if (tb == 0 and fc == 0) else [])
            vlast = None
            for tb in range(4):
                dstp = psA if tb % 2 == 0 else psB
                half = (tb // 2) * 512
                vlast = V(lambda e, tb=tb, dstp=dstp, half=half: e.tensor_scalar(
                    out=evac_to[dst_rows, tb * 512:(tb + 1) * 512],
                    in0=dstp[0:(dst_rows.stop - dst_rows.start), half:half + 512],
                    scalar1=gvec[0:(dst_rows.stop - dst_rows.start), scale_col:scale_col + 1],
                    scalar2=None, op0=ALU.mult, op1=ALU.bypass),
                    waits=[(TS, tprev), (VS, p.now("VS"))] if tb == 0 else [])
            return vlast

        def build_kv(kv):
            vK = build_wT(io["wk"], kv * 64, 64, extra_waits=[(VS, p.now("VS"))])
            vKd = project(64, slice(0, 64), 1, kdup, vK)
            dK1 = p.dma("sync", kdup[64:128, :], kdup[0:64, :], waits=[(VS, vKd)], inc=LD)
            dK2 = p.dma("sync", khat[0:64, :], kdup[0:64, :], inc=LD)
            vV = build_wT(io["wv"], kv * 64, 64, extra_waits=[(VS, vKd)])
            vV2 = project(64, slice(0, 64), 1, vfmt, vV)
            tv = None
            for kpi in range(NKP):
                tv = T(lambda e, kpi=kpi: e.transpose(
                    psTb[:, kpi * 64:(kpi + 1) * 64], vfmt[0:64, kpi * P:(kpi + 1) * P],
                    identb[0:64, 0:64]),
                    waits=[(VS, vV2), (VS, p.now("VS"))] if kpi == 0 else [])
            vvk = V(lambda e: e.tensor_copy(out=vkt[:, :], in_=psTb[:, 0:NKP * 64]),
                    waits=[(TS, tv)])
            sKq = SC(lambda e: e.activation(QSQ[0:64, :], kdup[0:64, :], AF.Square,
                                            bias=0.0, scale=1.0),
                     waits=[(VS, vKd), (LD, p.now("LD")), (SS, p.now("SS"))])
            vkm = V(lambda e: e.memset(sc1[0:1, 28:29], 0.0))
            for tb in range(4):
                tn = T(lambda e, tb=tb: e.matmul(
                    psC[0:1, 0:512], ind64[:, :], QSQ[0:64, tb * 512:(tb + 1) * 512],
                    start=True, stop=True, skip_group_check=True),
                    waits=[(SS, sKq), (VS, p.now("VS"))] if tb == 0 else [(VS, p.now("VS"))])
                V(lambda e: e.tensor_reduce(out=sc1[0:1, 29:30], in_=psC[0:1, 0:512],
                                            axis=AX.X, op=ALU.max), waits=[(TS, tn)])
                pad()
                V(lambda e: e.tensor_tensor(out=sc1[0:1, 28:29], in0=sc1[0:1, 28:29],
                                            in1=sc1[0:1, 29:30], op=ALU.max))
                vkm = pad()
            V(lambda e: e.memset(QSQ[0:1, 0:S], 0.0), waits=[(TS, p.now("TS"))])
            V(lambda e: e.memset(ETL[0][0:1, 0:1024], 0.0))
            V(lambda e: e.memset(ETL[1][0:1, 0:1024], 0.0))
            vmk = None
            for kpi in range(NKP):
                V(lambda e, kpi=kpi: e.memset(QSQ[0:1, kpi * P:kpi * P + 64], 1.0))
                c0 = kpi * P + 64
                ei, off = (0, c0) if c0 < 1024 else (1, c0 - 1024)
                vmk = V(lambda e, ei=ei, off=off: e.memset(ETL[ei][0:1, off:off + 64], 1.0))
            dm1 = p.dma("sync", khat[64:65, :], QSQ[0:1, 0:S], waits=[(VS, vmk)], inc=LD)
            dm2a = p.dma("sync", khat[65:66, 0:1024], ETL[0][0:1, 0:1024], inc=LD)
            dmask = p.dma("sync", khat[65:66, 1024:2048], ETL[1][0:1, 0:1024], inc=LD)
            return vvk, vkm, dmask

        for pr in range(NPAIR):
            kv = pr // 2
            if pr % 2 == 0:
                vvk, v_km, dmask_kv = build_kv(kv)
            vWq = build_wT(io["wq"], pr * P, P, extra_waits=[(VS, p.now("VS"))])
            vqp = project(P, slice(0, P), 0, qpair, vWq)
            sQq = SC(lambda e: e.activation(QSQ[:, :], qpair[:, :], AF.Square, bias=0.0,
                                            scale=1.0), waits=[(VS, vqp), (SS, p.now("SS"))])
            vqm = V(lambda e: e.memset(sc[0:2, 31:32], 0.0))
            for tb in range(4):
                tn = T(lambda e, tb=tb: e.matmul(
                    psC[0:2, 0:512], indAB[:, :], QSQ[:, tb * 512:(tb + 1) * 512],
                    start=True, stop=True, skip_group_check=True),
                    waits=[(SS, sQq), (VS, p.now("VS"))] if tb == 0 else [(VS, p.now("VS"))])
                V(lambda e: e.tensor_reduce(out=sc[0:2, 30:31], in_=psC[0:2, 0:512],
                                            axis=AX.X, op=ALU.max), waits=[(TS, tn)])
                pad()
                V(lambda e: e.tensor_tensor(out=sc[0:2, 31:32], in0=sc[0:2, 31:32],
                                            in1=sc[0:2, 30:31], op=ALU.max))
                vqm = pad()
            tq = T(lambda e: e.transpose(psC[0:1, 0:P], sc[:, 31:32], ident[:, :]),
                   waits=[(VS, vqm)])
            vM1 = V(lambda e: e.tensor_scalar(out=sc1[0:1, 4:6], in0=psC[0:1, 0:2],
                                              scalar1=sc1[0:1, 28:29], scalar2=None,
                                              op0=ALU.add, op1=ALU.bypass),
                    waits=[(TS, tq), (VS, v_km)])
            pad()
            vM2 = V(lambda e: e.tensor_scalar(out=sc1[0:1, 4:6], in0=sc1[0:1, 4:6],
                                              scalar1=-1.0 / 16.0, scalar2=None,
                                              op0=ALU.mult, op1=ALU.bypass))
            vM2 = pad()
            vb0 = bcast128(ebias[:, 0:1], sc1[0:1, 4:5], vM2, psC[:, 646:647], wait_s=p.now("SS"))
            vb1 = bcast128(ebias[:, 1:2], sc1[0:1, 5:6], vM2, psC[:, 647:648], wait_s=p.now("SS"))
            V(lambda e: e.tensor_tensor(out=ebias[:, 2:3], in0=ebias[:, 0:1],
                                        in1=ln32t[:, :], op=ALU.subtract),
              waits=[(VS, max(vb0, vb1)), (GD, G_CONST)])
            vb3 = V(lambda e: e.tensor_tensor(out=ebias[:, 3:4], in0=ebias[:, 1:2],
                                              in1=ln32t[:, :], op=ALU.subtract))
            vb3 = pad()
            for h01 in range(2):
                src = qpair[64 * h01:64 * (h01 + 1), :]
                for par in range(2):
                    p.dma("sync", QHAT[2 * h01 + par][0:64, :], src, waits=[(VS, vqp)], inc=LD)
            LD_Q = p.now("LD")

            EBUF = [E1A, E1B, ETL[0], ETL[1]]
            for qh in range(2):
                q0 = qh * 1024
                exp_done = {}

                def emit_E(n):
                    kp, h01 = n // 2, n % 2
                    g = 2 * h01 + (kp % 2)
                    mt = maskE[:, (kp // 2) * 32:(kp // 2) * 32 + 32]
                    e1 = EBUF[n % 4]
                    for qb in range(2):
                        T(lambda e, g=g, mt=mt, e1=e1, qb=qb, kp=kp: e.matmul(
                            psD[32 * g:32 * (g + 1), qb * 512:(qb + 1) * 512], mt,
                            e1[:, qb * 512:(qb + 1) * 512],
                            start=(kp < 2), stop=(kp >= NKP - 2),
                            tile_position=(0, 32 * g), skip_group_check=True),
                            waits=([(SS, exp_done[n])] if qb == 0 else []) +
                                  ([(VS, p.now("VS"))] if n == 0 and qb == 0 else []))

                for n in range(2 * NKP):
                    kp, h01 = n // 2, n % 2
                    if n >= 3:
                        emit_E(n - 3)
                    slp = PSLOT[n % 3]
                    r0, r1 = 64 * h01, 64 * (h01 + 1)
                    tpos = (64 * h01, 0)
                    wa = [(LD, LD_Q), (VS, p.now("VS"))] if n <= 2 else []
                    T(lambda e, kp=kp, slp=slp, q0=q0, r0=r0, r1=r1, tpos=tpos: e.matmul(
                        slp[:, 0:512], kdup[r0:r1, kp * P:(kp + 1) * P],
                        qpair[r0:r1, q0:q0 + 512],
                        start=True, stop=True, tile_position=tpos, skip_group_check=True),
                        waits=wa)
                    t1 = T(lambda e, kp=kp, slp=slp, q0=q0, r0=r0, r1=r1, tpos=tpos: e.matmul(
                        slp[:, 512:1024], kdup[r0:r1, kp * P:(kp + 1) * P],
                        qpair[r0:r1, q0 + 512:q0 + 1024],
                        start=True, stop=True, tile_position=tpos, skip_group_check=True))
                    exp_done[n] = SC(lambda e, slp=slp, h01=h01, n=n: e.activation(
                        EBUF[n % 4][:, :], slp[:, :], AF.Exp,
                        bias=ebias[:, h01:h01 + 1], scale=1.0),
                        waits=[(TS, t1), (VS, vb3)] if n == 0 else [(TS, t1)])
                for n in range(2 * NKP - 3, 2 * NKP):
                    emit_E(n)
                T_E = p.now("TS")
                sLn = SC(lambda e: e.activation(LNE[:, :], psD[:, :], AF.Ln, bias=0.0, scale=1.0),
                         waits=[(TS, T_E)])
                V(lambda e, q0=q0: e.tensor_scalar(out=bhi[:, q0:q0 + 1024], in0=LNE[:, :],
                                                   scalar1=-1.0, scalar2=None,
                                                   op0=ALU.mult, op1=ALU.bypass),
                  waits=[(SS, sLn)])
            V_BHI = p.now("VS")

            for qh in range(2):
                q0 = qh * 1024
                prev_s2 = {}
                exp2_done = {}
                db_done = {}
                av_t = None

                def emit_db(n):
                    kp, h01 = n // 2, n % 2
                    reg_i = 2 * h01 + (kp % 2)
                    g = reg_i
                    db_done[n] = p.dma(
                        "sync", QHAT[reg_i][64:66, :],
                        bhi[32 * g + 2 * (kp // 2):32 * g + 2 * (kp // 2) + 2, :],
                        waits=[(VS, V_BHI), (TS, prev_s2.get(reg_i, 0))], inc=LD)

                def emit_av(n):
                    kp, h01 = n // 2, n % 2
                    etl = EBUF[n % 4]
                    for qb in range(2):
                        nonlocal_av = T(lambda e, kp=kp, h01=h01, qb=qb, etl=etl: e.matmul(
                            psD[64 * h01:64 * (h01 + 1), qb * 512:(qb + 1) * 512],
                            vkt[:, kp * 64:(kp + 1) * 64],
                            etl[:, qb * 512:(qb + 1) * 512],
                            start=(kp == 0), stop=(kp == NKP - 1),
                            tile_position=(0, 64 * h01), skip_group_check=True),
                            waits=([(SS, exp2_done[n])] if qb == 0 else []) +
                                  ([(VS, p.now("VS"))] if n == 0 and qb == 0 else []))
                    return nonlocal_av

                emit_db(0)
                emit_db(1)
                for n in range(2 * NKP):
                    kp, h01 = n // 2, n % 2
                    reg_i = 2 * h01 + (kp % 2)
                    reg = QHAT[reg_i]
                    if n + 2 < 2 * NKP:
                        emit_db(n + 2)
                    if n >= 2:
                        av_t = emit_av(n - 2)
                    slp = PSLOT[n % 3]
                    T(lambda e, kp=kp, reg=reg, slp=slp, q0=q0: e.matmul(
                        slp[:, 0:512], khat[:, kp * P:(kp + 1) * P], reg[:, q0:q0 + 512],
                        start=True, stop=True, skip_group_check=True),
                        waits=[(LD, db_done[n])] + ([(VS, p.now("VS"))] if n == 0 else []))
                    t1 = T(lambda e, kp=kp, reg=reg, slp=slp, q0=q0: e.matmul(
                        slp[:, 512:1024], khat[:, kp * P:(kp + 1) * P],
                        reg[:, q0 + 512:q0 + 1024],
                        start=True, stop=True, skip_group_check=True))
                    prev_s2[reg_i] = t1
                    exp2_done[n] = SC(lambda e, slp=slp, h01=h01, n=n: e.activation(
                        EBUF[n % 4][:, :], slp[:, :], AF.Exp,
                        bias=ebias[:, 2 + h01:3 + h01], scale=1.0), waits=[(TS, t1)])
                for n in range(2 * NKP - 2, 2 * NKP):
                    av_t = emit_av(n)
                va = V(lambda e: e.tensor_copy(out=ASTG[:, :], in_=psD[:, :]),
                       waits=[(TS, av_t), (LD, p.now("LD"))])
                p.dma("sync", aspill.ap()[(b * NPAIR + pr) * P:(b * NPAIR + pr + 1) * P,
                                          q0:q0 + 1024],
                      ASTG[:, :], waits=[(VS, va)], inc=LD)
        LD_ATTN = p.now("LD")

        for r in range(8):
            l = p.dma("sync", FSTG[:, :], io["wo"][r * P:(r + 1) * P, :],
                      waits=([(LD, LD_ATTN)] if r == 0 else []) + [(VS, p.now("VS"))], inc=LD)
            vv = V(lambda e: e.tensor_copy(out=QITM[:, :], in_=FSTG[:, :]),
                   waits=[(LD, l), (VS, p.now("VS"))])
            tt = None
            for fc in range(8):
                tt = T(lambda e, fc=fc: e.transpose(psTb[:, fc * P:(fc + 1) * P],
                                                    QITM[:, fc * P:(fc + 1) * P], identb[:, :]),
                       waits=[(VS, vv), (VS, p.now("VS"))] if fc == 0 else [])
            V(lambda e, r=r: e.tensor_copy(
                out=WTv()[:, :, r * P:(r + 1) * P],
                in_=psTb[:, 0:1024].rearrange("p (c n) -> p c n", c=8, n=P)),
                waits=[(TS, tt)])
        V_WO = p.now("VS")

        def obank(oc):
            return [psA, psB, psC, psD][oc // 2][:, (oc % 2) * 512:(oc % 2 + 1) * 512]

        for qb in range(4):
            lds = None
            for pr in range(NPAIR):
                lds = p.dma("sync", OSTG[:, pr * 512:(pr + 1) * 512],
                            aspill.ap()[(b * NPAIR + pr) * P:(b * NPAIR + pr + 1) * P,
                                        qb * 512:(qb + 1) * 512],
                            waits=[(VS, V_WO)] if (qb == 0 and pr == 0) else [], inc=LD)
            tt = None
            for pr in range(NPAIR):
                for oc in range(8):
                    tt = T(lambda e, pr=pr, oc=oc: e.matmul(
                        obank(oc), WTv()[:, pr, oc * P:(oc + 1) * P],
                        OSTG[:, pr * 512:(pr + 1) * 512],
                        start=(pr == 0), stop=(pr == NPAIR - 1), skip_group_check=True),
                        waits=([(LD, lds), (VS, V_WO), (VS, p.now("VS"))]
                               if (pr == 0 and oc == 0) else []))
            for oc in [7, 6, 0, 1, 2, 3, 4, 5]:
                vv = V(lambda e, oc=oc: e.tensor_copy(out=ASTG[:, 0:512], in_=obank(oc)),
                       waits=[(TS, tt)] if oc == 7 else [(TS, p.now("TS"))])
                t2 = None
                for blk in range(4):
                    t2 = T(lambda e, blk=blk: e.transpose(
                        psTb[:, 1024 + blk * P:1024 + (blk + 1) * P],
                        ASTG[:, blk * P:(blk + 1) * P], identb[:, :]),
                        waits=[(VS, vv)] if blk == 0 else [])
                for blk in range(4):
                    t = TPB * b + qb * 4 + blk
                    x_t = xt(t)
                    V(lambda e, oc=oc, blk=blk, x_t=x_t: e.tensor_tensor(
                        out=x_t[:, oc * P:(oc + 1) * P], in0=x_t[:, oc * P:(oc + 1) * P],
                        in1=psTb[:, 1024 + blk * P:1024 + (blk + 1) * P], op=ALU.add),
                        waits=[(TS, t2)] if blk == 0 else [])

    if "f1" in stages:
        ffn_pass(0)
    if "a0" in stages:
        attention(0)
    if "a1" in stages:
        attention(1)
    if "f2" in stages:
        ffn_pass(1)
    V_END = p.now("VS")
    if debug:
        dbgq = nc.dram_tensor("dbgq", [P, 2048], F32, kind="ExternalOutput").ap()
        p.dma("sync", dbgq[:, 0:1024], UF32[:, 3072:4096], waits=[(VS, V_END)], inc=LD)
        dbq2 = V(lambda e: e.tensor_copy(out=UF32[:, 5120:6144], in_=QI8))
        p.dma("sync", dbgq[:, 1024:2048], UF32[:, 5120:6144], waits=[(VS, dbq2)], inc=LD)
        dbg = nc.dram_tensor("dbg", [P, 192], F32, kind="ExternalOutput").ap()
        p.dma("sync", dbg[:, 0:32], sc[:, :], waits=[(VS, V_END)], inc=LD)
        p.dma("sync", dbg[0:1, 32:64], sc1[:, :], inc=LD)
        p.dma("sync", dbg[:, 64:96], statm[:, :], inc=LD)
        p.dma("sync", dbg[:, 96:128], statr[:, :], inc=LD)
        p.dma("sync", dbg[:, 128:160], stata[:, :], inc=LD)
        p.dma("sync", dbg[:, 160:192], statz[:, :], inc=LD)
    for t in range(NT):
        p.dma("sync", yio[t], xt(t), waits=[(VS, V_END)] if t == 0 else [], inc=LD)
    LD_END = p.now("LD")
    p.emit("sync", lambda e: e.nop(), waits=[(LD, LD_END)])

    p.build(stack)
    stack.close()
    return nc


_NC_CACHE = None


def kernel(L0_attn_hidden_states, L0_attn_w_q, L0_attn_w_k, L0_attn_w_v, L0_attn_w_o,
           L0_attn_w_kgen, L0_attn_b_kgen, L0_attn_w_conv, L0_attn_b_conv,
           L0_ffn_weight, L0_ffn_bias):
    global _NC_CACHE
    if _NC_CACHE is None:
        _NC_CACHE = build_nc()
    nc = _NC_CACHE
    x = np.ascontiguousarray(L0_attn_hidden_states, dtype=np.float32)
    shared = {
        "wq": np.ascontiguousarray(L0_attn_w_q, dtype=np.float32),
        "wk": np.ascontiguousarray(L0_attn_w_k, dtype=np.float32),
        "wv": np.ascontiguousarray(L0_attn_w_v, dtype=np.float32),
        "wo": np.ascontiguousarray(L0_attn_w_o, dtype=np.float32),
        "wkgen": np.ascontiguousarray(L0_attn_w_kgen, dtype=np.float32),
        "bkgen": np.ascontiguousarray(L0_attn_b_kgen, dtype=np.float32).reshape(1, 3),
        "wconv": np.ascontiguousarray(L0_attn_w_conv, dtype=np.float32).reshape(HID, 3),
        "bconv": np.ascontiguousarray(L0_attn_b_conv, dtype=np.float32).reshape(1, HID),
        "wffn": np.ascontiguousarray(L0_ffn_weight, dtype=np.float32),
        "bffn": np.ascontiguousarray(L0_ffn_bias, dtype=np.float32).reshape(1, HID),
    }
    in_maps = []
    for c in range(8):
        m = dict(shared)
        m["x"] = np.ascontiguousarray(x[2 * c:2 * c + 2].reshape(B * S, HID))
        in_maps.append(m)
    res = run_bass_kernel_spmd(nc, in_maps, core_ids=list(range(8)))
    out = np.concatenate([np.asarray(r["y"]).reshape(B, S, HID) for r in res.results], axis=0)
    return out.astype(np.float32)


# version 08
# speedup vs baseline: 1.3051x; 1.0784x over previous
import math
import contextlib
import numpy as np
import ml_dtypes
import concourse.bass as bass
import concourse.mybir as mybir
from concourse.bass_utils import run_bass_kernel_spmd

F32 = mybir.dt.float32
BF16 = mybir.dt.bfloat16
I8 = mybir.dt.int8
AF = mybir.ActivationFunctionType
ALU = mybir.AluOpType
AX = mybir.AxisListType

P = 128
B = 2
S = 2048
HID = 1024
NT = 32
TPB = 16
NH = 16
HD = 64
NPAIR = 8
NKP = 16
LN32 = math.log(32.0 + 1e-9)

ENGINES = ["sync", "scalar", "vector", "tensor", "gpsimd"]


class Prog:
    def __init__(self, nc):
        self.nc = nc
        self.ops = {e: [] for e in ENGINES}
        self.counts = {}
        self.handles = {}
        self.owner = {}

    def sem(self, name, owner):
        self.counts[name] = 0
        self.owner[name] = owner
        return name

    def now(self, name):
        return self.counts[name]

    def emit(self, eng, fn, waits=(), inc=None, inc_n=1):
        if inc is not None:
            assert self.owner[inc] == eng, (inc, eng)
            self.counts[inc] += inc_n
        self.ops[eng].append((fn, tuple(waits), inc, inc_n))
        return self.counts[inc] if inc is not None else None

    def dma(self, eng, dst, src, waits=(), inc=None, slow=False):
        return self.emit(eng, (lambda d, s, sl: (lambda e: e.dma_start(
            out=d, in_=s, allow_slow_non_contiguous=sl)))(dst, src, slow),
            waits=waits, inc=inc, inc_n=16)

    def build(self, stack):
        nc = self.nc
        for name in self.counts:
            self.handles[name] = stack.enter_context(nc.semaphore(name))
        with nc.Block() as block:
            def mk(e):
                def run(eng):
                    for fn, waits, inc, inc_n in self.ops[e]:
                        for s, v in waits:
                            if v > 0:
                                eng.wait_ge(self.handles[s], v)
                        ins = fn(eng)
                        if inc is not None:
                            ins.then_inc(self.handles[inc], inc_n)
                return run
            for e in ENGINES:
                if self.ops[e]:
                    getattr(block, e)(mk(e))


def build_nc(stages=("f1", "a0", "a1", "f2"), debug=False):
    nc = bass.Bass("TRN2", target_bir_lowering=False)

    io = {}
    io["x"] = nc.dram_tensor("x", [B * S, HID], F32, kind="ExternalInput").ap()
    io["wq"] = nc.dram_tensor("wq", [HID, HID], F32, kind="ExternalInput").ap()
    io["wk"] = nc.dram_tensor("wk", [256, HID], F32, kind="ExternalInput").ap()
    io["wv"] = nc.dram_tensor("wv", [256, HID], F32, kind="ExternalInput").ap()
    io["wo"] = nc.dram_tensor("wo", [HID, HID], F32, kind="ExternalInput").ap()
    io["wkgen"] = nc.dram_tensor("wkgen", [3, HID], F32, kind="ExternalInput").ap()
    io["bkgen"] = nc.dram_tensor("bkgen", [1, 3], F32, kind="ExternalInput").ap()
    io["wconv"] = nc.dram_tensor("wconv", [HID, 3], F32, kind="ExternalInput").ap()
    io["bconv"] = nc.dram_tensor("bconv", [1, HID], F32, kind="ExternalInput").ap()
    io["wffn"] = nc.dram_tensor("wffn", [HID, HID], F32, kind="ExternalInput").ap()
    io["bffn"] = nc.dram_tensor("bffn", [1, HID], F32, kind="ExternalInput").ap()
    io["y"] = nc.dram_tensor("y", [B * S, HID], F32, kind="ExternalOutput").ap()

    cc_in = [nc.dram_tensor(f"cc_in{i}", [1, 8], F32) for i in range(2)]
    cc_out = [nc.dram_tensor(f"cc_out{i}", [1, 8], F32) for i in range(2)]
    aspill = nc.dram_tensor("aspill", [B * NPAIR * P, S], BF16)

    p = Prog(nc)
    LD = p.sem("LD", "sync")
    GD = p.sem("GD", "gpsimd")
    TS = p.sem("TS", "tensor")
    VS = p.sem("VS", "vector")
    SS = p.sem("SS", "scalar")

    stack = contextlib.ExitStack()
    sb = lambda name, shape, dt: stack.enter_context(nc.sbuf_tensor(name, shape, dt))
    ps = lambda name, shape, dt: stack.enter_context(nc.psum_tensor(name, shape, dt))

    x_sb = sb("x_sb", [P, NT * HID], F32)
    ufm = sb("ufm", [P, 8 * S], BF16)
    pool = sb("pool", [P, 26624], mybir.dt.uint8)
    bhi = sb("bhi", [P, S], BF16)
    kdup = sb("kdup", [P, S], BF16)
    khat = sb("khat", [66, S], BF16)
    vkt = sb("vkt", [P, NKP * HD], BF16)
    qpair = sb("qpair", [P, S], BF16)
    maskE = sb("maskE", [P, 8 * 32], BF16)
    statm = sb("statm", [P, NT], F32)
    statr = sb("statr", [P, NT], F32)
    stata = sb("stata", [P, NT], F32)
    statb = sb("statb", [P, NT], F32)
    statz = sb("statz", [P, NT], F32)
    ident = sb("ident", [P, P], F32)
    identb = sb("identb", [P, P], BF16)
    ones_r = sb("ones_r", [1, P], F32)
    ones_rb = sb("ones_rb", [1, P], BF16)
    ind64 = sb("ind64", [64, 1], BF16)
    indAB = sb("indAB", [P, 2], BF16)
    sc = sb("sc", [P, 32], F32)
    sc1 = sb("sc1", [1, 32], F32)
    stage = None
    convw = sb("convw", [P, 8 * 3], F32)
    convb = sb("convb", [P, 8], F32)
    ctxa = sb("ctxa", [P, 8], F32)
    ctxb = sb("ctxb", [P, 8], BF16)
    wkg = sb("wkg", [P, 8 * 3], BF16)
    gvec = sb("gvec", [P, 2], F32)
    ebias = sb("ebias", [P, 4], F32)
    ln32t = sb("ln32t", [P, 1], F32)
    padt = sb("padt", [P, 64], F32)

    vfmt = bhi[0:64, :]
    UF32 = ufm.bitcast(F32)
    UB16 = ufm
    BROW = UF32[0:1, 4096:5120]
    BROWB = UB16[0:1, 10240:11264]

    poolb = pool.bitcast(BF16)
    poolf = pool.bitcast(F32)
    tmpv = poolf[:, 4096:4128]
    tmpv2 = poolf[:, 4128:4160]

    WT = poolb[:, 0:8192]
    WTv = lambda: WT.rearrange("p (c n) -> p c n", c=8, n=1024)
    FSTG = poolf[:, 4096:5120]
    QITM = poolb[:, 10240:11264]
    QIFM = poolb[:, 11264:12288]
    QI8 = pool.bitcast(I8)[:, 24576:25600]
    Z2R = [poolb[:, 0:4112], poolb[:, 4112:8224]]
    z2v = lambda i: Z2R[i].rearrange("p (c n) -> p c n", c=8, n=514)
    Z2TM = poolf[:, 4112 // 2 + 2056:0] if False else poolf[:, 4112:5136]
    CVT0 = poolf[:, 5136:5648]
    CVT1 = poolf[:, 5648:6160]
    QHAT = [poolb[0:66, 0:2048], poolb[0:66, 2048:4096],
            poolb[0:66, 4096:6144], poolb[0:66, 6144:8192]]
    E1A = poolb[:, 8192:9216]
    E1B = poolb[:, 9216:10240]
    ETL = [poolb[:, 10240:11264], poolb[:, 11264:12288]]
    ASTG = poolb[:, 12288:13312]
    wslice = ASTG
    LNE = poolf[:, 4096:5120]
    QSQ = poolb[:, 8192:10240]
    OSTG = poolb[:, 8192:12288]
    ufmv = lambda: ufm.rearrange("p (c n) -> p c n", c=8, n=S)
    convwv = convw.rearrange("p (c k) -> p c k", c=8, k=3)

    psA = ps("psA", [P, 1024], F32)
    psB = ps("psB", [P, 1024], F32)
    psC = ps("psC", [P, 1024], F32)
    psD = ps("psD", [P, 1024], F32)
    PSLOT = [psA, psB, psC]
    psTb = psD.bitcast(BF16)
    psCb = psC.bitcast(BF16)

    def T(fn, waits=()):
        return p.emit("tensor", fn, waits=waits, inc=TS)

    def V(fn, waits=()):
        return p.emit("vector", fn, waits=waits, inc=VS)

    def SC(fn, waits=()):
        return p.emit("scalar", fn, waits=waits, inc=SS)

    def G(fn, waits=()):
        return p.emit("gpsimd", fn, waits=waits, inc=GD, inc_n=1)

    def pad():
        V(lambda e: e.memset(padt[:, :], 0.0))
        return V(lambda e: e.memset(padt[:, :], 1.0))

    def bcast128(dst_col, src11, wait_v, psum_slice, wait_s=0):
        t = T(lambda e, s=src11, o=psum_slice: e.matmul(
            o, ones_r[0:1, :], s, start=True, stop=True, skip_group_check=True),
            waits=[(VS, wait_v), (SS, wait_s)])
        V(lambda e, o=psum_slice, d=dst_col: e.tensor_copy(out=d, in_=o),
          waits=[(TS, t)])
        return pad()

    xt = lambda t: x_sb[:, t * HID:(t + 1) * HID]
    xio = io["x"].rearrange("(n p) h -> n p h", p=P)
    yio = io["y"].rearrange("(n p) h -> n p h", p=P)
    for t in range(NT):
        p.dma("sync", xt(t), xio[t], inc=LD)
    LD_X = p.now("LD")

    p.dma("sync", convw.rearrange("p (c k) -> p c k", c=8, k=3),
          io["wconv"].rearrange("(c p) k -> p c k", p=P), inc=LD)
    p.dma("sync", convb.rearrange("p (c o) -> p c o", c=8, o=1),
          io["bconv"].rearrange("o (c p) -> p c o", p=P), inc=LD, slow=True)
    p.dma("sync", FSTG[0:3, 0:1024], io["wkgen"], inc=LD)
    LD_WKG = p.now("LD")

    G(lambda e: e.memset(ident[:, :], 1.0))
    g_id = G(lambda e: e.affine_select(ident[:, :], ident[:, :], [[1, P]], ALU.is_equal, 0.0,
                                       base=0, channel_multiplier=-1))
    v_idb = V(lambda e: e.tensor_copy(out=identb[:, :], in_=ident[:, :]), waits=[(GD, g_id)])
    v_wkgc = V(lambda e: e.tensor_copy(out=QITM[0:3, 0:1024], in_=FSTG[0:3, 0:1024]),
               waits=[(LD, LD_WKG)])
    t_wkg = None
    for c in range(8):
        t_wkg = T(lambda e, c=c: e.transpose(psTb[:, c * 4:c * 4 + 3],
                                             QITM[0:3, c * P:(c + 1) * P], identb[0:3, 0:3]),
                  waits=[(VS, max(v_idb, v_wkgc))] if c == 0 else [])
    v_wkg = V(lambda e: e.tensor_copy(
        out=wkg.rearrange("p (c k) -> p c k", c=8, k=3),
        in_=psTb[:, 0:32].rearrange("p (c k) -> p c k", c=8, k=4)[:, :, 0:3]),
        waits=[(TS, t_wkg)])
    G(lambda e: e.memset(ones_r[:, :], 1.0))
    G(lambda e: e.memset(ones_rb[:, :], 1.0))
    G(lambda e: e.memset(ind64[:, :], 1.0))
    G(lambda e: e.memset(indAB[:, :], 0.0))
    G(lambda e: e.memset(indAB[0:64, 0:1], 1.0))
    G(lambda e: e.memset(indAB[64:128, 1:2], 1.0))
    G(lambda e: e.memset(ln32t[:, :], LN32))
    G(lambda e: e.memset(maskE[:, :], 0.0))
    for j in range(8):
        G(lambda e, j=j: e.memset(maskE[0:64, j * 32 + 2 * j: j * 32 + 2 * j + 1], 1.0))
        G(lambda e, j=j: e.memset(maskE[64:128, j * 32 + 2 * j + 1: j * 32 + 2 * j + 2], 1.0))
    G_CONST = p.now("GD")

    def ffn_pass(idx):
        for t in range(NT):
            x_t = xt(t)
            va = V(lambda e, x_t=x_t: e.tensor_tensor(out=UF32[:, 2048:3072], in0=x_t, in1=x_t,
                                                      op=ALU.mult),
                   waits=[(LD, LD_X)] if t == 0 else [])
            V(lambda e, t=t, x_t=x_t: e.tensor_reduce(out=statm[:, t:t + 1], in_=x_t,
                                                      axis=AX.X, op=ALU.add))
            V(lambda e, t=t: e.tensor_reduce(out=statr[:, t:t + 1], in_=UF32[:, 2048:3072],
                                             axis=AX.X, op=ALU.add))
            V(lambda e, t=t, x_t=x_t: e.tensor_reduce(out=tmpv[:, t:t + 1], in_=x_t,
                                                      axis=AX.X, op=ALU.max))
            V(lambda e, t=t, x_t=x_t: e.tensor_reduce(out=tmpv2[:, t:t + 1], in_=x_t,
                                                      axis=AX.X, op=ALU.min))
        V(lambda e: e.tensor_scalar(out=statm[:, :], in0=statm[:, :], scalar1=1.0 / HID,
                                    scalar2=None, op0=ALU.mult, op1=ALU.bypass))
        V(lambda e: e.tensor_scalar(out=statr[:, :], in0=statr[:, :], scalar1=1.0 / HID,
                                    scalar2=None, op0=ALU.mult, op1=ALU.bypass))
        pad()
        V(lambda e: e.tensor_tensor(out=statz[:, :], in0=statm[:, :], in1=statm[:, :],
                                    op=ALU.mult))
        pad()
        V(lambda e: e.tensor_tensor(out=statr[:, :], in0=statr[:, :], in1=statz[:, :],
                                    op=ALU.subtract))
        pad()
        V(lambda e: e.tensor_scalar(out=statr[:, :], in0=statr[:, :], scalar1=1.0,
                                    scalar2=1e-5, op0=ALU.mult, op1=ALU.add))
        pad()
        vr0 = V(lambda e: e.reciprocal(out=statz[:, :], in_=statr[:, :]))
        vr0 = pad()
        ss0 = SC(lambda e: e.activation(stata[:, :], statz[:, :], AF.Sqrt, bias=0.0, scale=1.0),
                 waits=[(VS, vr0)])
        V(lambda e: e.tensor_tensor(out=statb[:, :], in0=stata[:, :], in1=stata[:, :],
                                    op=ALU.mult), waits=[(SS, ss0)])
        pad()
        V(lambda e: e.tensor_tensor(out=statb[:, :], in0=statb[:, :], in1=statr[:, :],
                                    op=ALU.mult))
        pad()
        V(lambda e: e.tensor_scalar(out=statb[:, :], in0=statb[:, :], scalar1=-0.5,
                                    scalar2=1.5, op0=ALU.mult, op1=ALU.add))
        pad()
        V(lambda e: e.tensor_tensor(out=statr[:, :], in0=stata[:, :], in1=statb[:, :],
                                    op=ALU.mult))
        pad()
        V(lambda e: e.tensor_tensor(out=tmpv[:, :], in0=tmpv[:, :], in1=statm[:, :],
                                    op=ALU.subtract))
        V(lambda e: e.tensor_tensor(out=tmpv2[:, :], in0=statm[:, :], in1=tmpv2[:, :],
                                    op=ALU.subtract))
        pad()
        V(lambda e: e.tensor_tensor(out=tmpv[:, :], in0=tmpv[:, :], in1=tmpv2[:, :],
                                    op=ALU.max))
        pad()
        V(lambda e: e.tensor_tensor(out=statz[:, :], in0=tmpv[:, :], in1=statr[:, :],
                                    op=ALU.mult))
        pad()
        vz5 = V(lambda e: e.tensor_reduce(out=sc[:, 4:5], in_=statz[:, :], axis=AX.X, op=ALU.max))
        tz = T(lambda e: e.transpose(psC[0:1, 0:P], sc[:, 4:5], ident[:, :]),
               waits=[(VS, vz5), (GD, G_CONST)])
        vz6 = V(lambda e: e.tensor_reduce(out=sc1[0:1, 4:5], in_=psC[0:1, 0:P], axis=AX.X,
                                          op=ALU.max), waits=[(TS, tz)])
        vz6 = pad()
        ts8 = T(lambda e: e.matmul(psC[0:1, 512:520], sc1[0:1, 4:5], ones_r[0:1, 0:8],
                                   start=True, stop=True, skip_group_check=True),
                waits=[(VS, vz6)])
        vs8 = V(lambda e: e.tensor_copy(out=sc1[0:1, 16:24], in_=psC[0:1, 512:520]), waits=[(TS, ts8)])
        g1 = p.dma("gpsimd", cc_in[idx].ap(), sc1[0:1, 16:24], waits=[(VS, vs8)], inc=GD)
        g2 = G(lambda e: e.collective_compute(
            "AllReduce", mybir.AluOpType.max, replica_groups=[list(range(8))],
            ins=[cc_in[idx].ap()], outs=[cc_out[idx].ap()]), waits=[(GD, g1)])
        g3 = p.dma("gpsimd", sc1[0:1, 5:6], cc_out[idx].ap()[0:1, 0:1], waits=[(GD, g2)], inc=GD)
        v_i = V(lambda e: e.memset(sc[:, 0:1], 0.0))
        for r in range(8):
            l = p.dma("sync", FSTG[:, :], io["wffn"][r * P:(r + 1) * P, :],
                      waits=([(VS, v_wkg)] if (idx == 0 and r == 0) else
                             ([(VS, p.now("VS"))] if r == 0 else [])) +
                            [(SS, p.now("SS"))], inc=LD)
            s_abs = SC(lambda e: e.activation(UF32[:, 0:1024], FSTG[:, :], AF.Abs,
                                              bias=0.0, scale=1.0),
                       waits=[(LD, l), (VS, p.now("VS"))])
            v_r = V(lambda e: e.tensor_reduce(out=sc[:, 1:2], in_=UF32[:, 0:1024], axis=AX.X,
                                              op=ALU.add), waits=[(SS, s_abs)])
            pad()
            v_i = V(lambda e: e.tensor_tensor(out=sc[:, 0:1], in0=sc[:, 0:1], in1=sc[:, 1:2],
                                              op=ALU.add))
        t_tr = T(lambda e: e.transpose(psC[0:1, 0:P], sc[:, 0:1], ident[:, :]),
                 waits=[(VS, v_i), (GD, G_CONST)])
        v_s = V(lambda e: e.tensor_reduce(out=sc1[0:1, 0:1], in_=psC[0:1, 0:P], axis=AX.X,
                                          op=ALU.add), waits=[(TS, t_tr)])
        pad()
        v_thr = V(lambda e: e.tensor_scalar(out=sc1[0:1, 1:2], in0=sc1[0:1, 0:1],
                                            scalar1=0.5 / (1024.0 * 1024.0), scalar2=None,
                                            op0=ALU.mult, op1=ALU.bypass))
        v_ws = V(lambda e: e.tensor_scalar(out=sc1[0:1, 2:3], in0=sc1[0:1, 0:1],
                                           scalar1=1.0 / (1024.0 * 1024.0), scalar2=None,
                                           op0=ALU.mult, op1=ALU.bypass))
        v_thrb = bcast128(sc[:, 2:3], sc1[0:1, 1:2], v_thr, psC[:, 512:513])
        v_thrn = V(lambda e: e.tensor_scalar(out=sc[:, 3:4], in0=sc[:, 2:3], scalar1=-1.0,
                                             scalar2=None, op0=ALU.mult, op1=ALU.bypass))
        for r in range(8):
            l = p.dma("sync", FSTG[:, :], io["wffn"][r * P:(r + 1) * P, :],
                      waits=([(SS, p.now("SS"))] if r == 0 else []) + [(VS, p.now("VS"))],
                      inc=LD)
            v1 = V(lambda e: e.tensor_scalar(out=UF32[:, 0:1024], in0=FSTG[:, :],
                                             scalar1=sc[:, 2:3], scalar2=None,
                                             op0=ALU.is_gt, op1=ALU.bypass),
                   waits=[(LD, l), (VS, v_thrn)])
            v2 = V(lambda e: e.tensor_scalar(out=UF32[:, 1024:2048], in0=FSTG[:, :],
                                             scalar1=sc[:, 3:4], scalar2=None,
                                             op0=ALU.is_lt, op1=ALU.bypass))
            v3 = V(lambda e: e.tensor_tensor(out=QITM[:, :], in0=UF32[:, 0:1024],
                                             in1=UF32[:, 1024:2048], op=ALU.subtract))
            tt = None
            for fc in range(8):
                tt = T(lambda e, fc=fc: e.transpose(psTb[:, fc * P:(fc + 1) * P],
                                                    QITM[:, fc * P:(fc + 1) * P], identb[:, :]),
                       waits=[(VS, max(v3, v_idb)), (VS, p.now("VS"))] if fc == 0 else [])
            V(lambda e, r=r: e.tensor_copy(
                out=WTv()[:, :, r * P:(r + 1) * P],
                in_=psTb[:, 0:1024].rearrange("p (c n) -> p c n", c=8, n=P)),
                waits=[(TS, tt)])
        V(lambda e: e.tensor_scalar(out=sc1[0:1, 6:7], in0=sc1[0:1, 5:6], scalar1=1.0 / 127.0,
                                    scalar2=None, op0=ALU.mult, op1=ALU.bypass),
          waits=[(GD, g3)])
        pad()
        vi2 = V(lambda e: e.reciprocal(out=sc1[0:1, 7:8], in_=sc1[0:1, 6:7]))
        pad()
        vi3 = V(lambda e: e.tensor_tensor(out=sc1[0:1, 8:9], in0=sc1[0:1, 2:3],
                                          in1=sc1[0:1, 6:7], op=ALU.mult))
        pad()
        vi4 = V(lambda e: e.tensor_scalar(out=sc1[0:1, 9:10], in0=sc1[0:1, 8:9], scalar1=0.5,
                                          scalar2=None, op0=ALU.mult, op1=ALU.bypass))
        vi5 = V(lambda e: e.reciprocal(out=sc1[0:1, 10:11], in_=sc1[0:1, 8:9]))
        vi5 = pad()
        lb = p.dma("sync", BROW, io["bffn"], waits=[(SS, p.now("SS"))], inc=LD)
        vi6 = V(lambda e: e.tensor_scalar(out=BROWB, in0=BROW, scalar1=sc1[0:1, 10:11],
                                          scalar2=None, op0=ALU.mult, op1=ALU.bypass),
                waits=[(LD, lb), (VS, vi5)])
        v_ri = bcast128(sc[:, 5:6], sc1[0:1, 7:8], vi2, psC[:, 520:521])
        v_w2 = bcast128(sc[:, 6:7], sc1[0:1, 9:10], vi4, psC[:, 521:522])
        V(lambda e: e.tensor_scalar(out=stata[:, :], in0=statr[:, :], scalar1=sc[:, 5:6],
                                    scalar2=None, op0=ALU.mult, op1=ALU.bypass),
          waits=[(VS, v_ri)])
        pad()
        V(lambda e: e.tensor_tensor(out=statb[:, :], in0=statm[:, :], in1=stata[:, :],
                                    op=ALU.mult))
        pad()
        V(lambda e: e.tensor_scalar(out=statb[:, :], in0=statb[:, :], scalar1=-1.0,
                                    scalar2=None, op0=ALU.mult, op1=ALU.bypass))
        va3 = pad()
        QIFM2 = UB16[:, 12288:13312]
        vq3_of = {}

        def emit_mm(t):
            out_ps = psA if t % 2 == 0 else psB
            src = QIFM if t % 2 == 0 else QIFM2
            tm = T(lambda e, o=out_ps: e.matmul(o[:, 0:512], ones_rb[0:1, :], BROWB[0:1, 0:512],
                                                start=True, stop=False, skip_group_check=True),
                   waits=[(VS, max(vq3_of[t], vi6))])
            tm = T(lambda e, o=out_ps: e.matmul(o[:, 512:1024], ones_rb[0:1, :],
                                                BROWB[0:1, 512:1024],
                                                start=True, stop=False, skip_group_check=True))
            for fc in range(8):
                for half in range(2):
                    tm = T(lambda e, fc=fc, half=half, o=out_ps, src=src: e.matmul(
                        o[:, half * 512:(half + 1) * 512],
                        src[:, fc * P:(fc + 1) * P],
                        WTv()[:, fc, half * 512:(half + 1) * 512],
                        start=False, stop=(fc == 7), skip_group_check=True))
            return tm

        def emit_evac(t, tm):
            x_t = xt(t)
            V(lambda e, o=(psA if t % 2 == 0 else psB): e.tensor_scalar(
                out=UF32[:, 3072:4096], in0=o[:, :], scalar1=sc[:, 6:7], scalar2=None,
                op0=ALU.mult, op1=ALU.bypass), waits=[(TS, tm), (VS, v_w2)])
            V(lambda e, x_t=x_t: e.tensor_tensor(out=x_t, in0=x_t, in1=UF32[:, 3072:4096],
                                                 op=ALU.add))

        tm_prev = None
        tt_prev = None
        for t in range(NT):
            x_t = xt(t)
            if t >= 1:
                tm_prev = emit_mm(t - 1)
            vq1 = V(lambda e, t=t, x_t=x_t: e.tensor_scalar(
                out=QI8, in0=x_t, scalar1=stata[:, t:t + 1], scalar2=statb[:, t:t + 1],
                op0=ALU.mult, op1=ALU.add),
                waits=[(VS, va3)] if t == 0 else [])
            vq2 = V(lambda e: e.tensor_copy(out=QITM[:, :], in_=QI8),
                    waits=[(TS, tt_prev)] if tt_prev else [])
            tt = None
            po = 1024 + (t % 2) * 0
            for fc in range(8):
                tt = T(lambda e, fc=fc: e.transpose(psTb[:, 1024 + fc * P:1024 + (fc + 1) * P],
                                                    QITM[:, fc * P:(fc + 1) * P], identb[:, :]),
                       waits=[(VS, vq2)] if fc == 0 else [])
            tt_prev = tt
            vq3_of[t] = V(lambda e, t=t: e.tensor_copy(
                out=(QIFM if t % 2 == 0 else QIFM2)[:, :], in_=psTb[:, 1024:2048]),
                waits=[(TS, tt)])
            if t == 0:
                continue
            emit_evac(t - 1, tm_prev)
        tm_prev = emit_mm(NT - 1)
        emit_evac(NT - 1, tm_prev)

    def attention(b):
        sl = slice(TPB * b, TPB * (b + 1))
        for i, t in enumerate(range(TPB * b, TPB * (b + 1))):
            x_t = xt(t)
            V(lambda e, x_t=x_t: e.tensor_tensor(out=Z2TM[:, :], in0=x_t, in1=x_t, op=ALU.mult),
              waits=[(VS, p.now("VS"))] if i == 0 else [])
            V(lambda e, t=t, x_t=x_t: e.tensor_reduce(out=statm[:, t:t + 1], in_=x_t,
                                                      axis=AX.X, op=ALU.add))
            V(lambda e, t=t: e.tensor_reduce(out=statr[:, t:t + 1], in_=Z2TM[:, :],
                                             axis=AX.X, op=ALU.add))
        V(lambda e: e.tensor_scalar(out=statm[:, sl], in0=statm[:, sl], scalar1=1.0 / HID,
                                    scalar2=None, op0=ALU.mult, op1=ALU.bypass))
        V(lambda e: e.tensor_scalar(out=statr[:, sl], in0=statr[:, sl], scalar1=1.0 / HID,
                                    scalar2=None, op0=ALU.mult, op1=ALU.bypass))
        pad()
        V(lambda e: e.tensor_tensor(out=statz[:, sl], in0=statm[:, sl], in1=statm[:, sl],
                                    op=ALU.mult))
        pad()
        V(lambda e: e.tensor_tensor(out=statr[:, sl], in0=statr[:, sl], in1=statz[:, sl],
                                    op=ALU.subtract))
        pad()
        V(lambda e: e.tensor_scalar(out=statr[:, sl], in0=statr[:, sl], scalar1=1.0,
                                    scalar2=1e-5, op0=ALU.mult, op1=ALU.add))
        pad()
        vr0 = V(lambda e: e.reciprocal(out=statz[:, sl], in_=statr[:, sl]))
        vr0 = pad()
        ss0 = SC(lambda e: e.activation(stata[:, sl], statz[:, sl], AF.Sqrt, bias=0.0, scale=1.0),
                 waits=[(VS, vr0)])
        V(lambda e: e.tensor_tensor(out=statb[:, sl], in0=stata[:, sl], in1=stata[:, sl],
                                    op=ALU.mult), waits=[(SS, ss0)])
        pad()
        V(lambda e: e.tensor_tensor(out=statb[:, sl], in0=statb[:, sl], in1=statr[:, sl],
                                    op=ALU.mult))
        pad()
        V(lambda e: e.tensor_scalar(out=statb[:, sl], in0=statb[:, sl], scalar1=-0.5,
                                    scalar2=1.5, op0=ALU.mult, op1=ALU.add))
        pad()
        V(lambda e: e.tensor_tensor(out=statr[:, sl], in0=stata[:, sl], in1=statb[:, sl],
                                    op=ALU.mult))
        pad()
        V(lambda e: e.tensor_tensor(out=statb[:, sl], in0=statm[:, sl], in1=statr[:, sl],
                                    op=ALU.mult))
        pad()
        V(lambda e: e.tensor_scalar(out=statb[:, sl], in0=statb[:, sl], scalar1=-1.0,
                                    scalar2=None, op0=ALU.mult, op1=ALU.bypass))
        vmb2 = pad()
        V(lambda e: e.memset(ctxa[:, :], 0.0))

        def produce_z2(j):
            slot = z2v(j % 2)
            for q in range(4):
                t = TPB * b + 4 * j + q
                x_t = xt(t)
                va = V(lambda e, t=t, x_t=x_t: e.tensor_scalar(
                    out=Z2TM[:, :], in0=x_t, scalar1=statr[:, t:t + 1], scalar2=statb[:, t:t + 1],
                    op0=ALU.mult, op1=ALU.add))
                tt = None
                for fc in range(8):
                    dst = psA if fc < 4 else psB
                    tt = T(lambda e, fc=fc, dst=dst: e.transpose(
                        dst[:, (fc % 4) * P:(fc % 4 + 1) * P], Z2TM[:, fc * P:(fc + 1) * P],
                        ident[:, :]),
                        waits=[(VS, p.now("VS"))] if fc == 0 else [])
                V(lambda e, q=q, slot=slot: e.tensor_copy(
                    out=slot[:, 0:4, 1 + q * P:1 + (q + 1) * P],
                    in_=psA[:, 0:512].rearrange("p (c n) -> p c n", c=4, n=P)), waits=[(TS, tt)])
                V(lambda e, q=q, slot=slot: e.tensor_copy(
                    out=slot[:, 4:8, 1 + q * P:1 + (q + 1) * P],
                    in_=psB[:, 0:512].rearrange("p (c n) -> p c n", c=4, n=P)))
            V(lambda e, slot=slot: e.tensor_reduce(out=sc[:, 16:24], in_=slot[:, :, 1:513],
                                                   axis=AX.X, op=ALU.add))
            pad()
            V(lambda e: e.tensor_tensor(out=ctxa[:, :], in0=ctxa[:, :], in1=sc[:, 16:24],
                                        op=ALU.add))
            pad()
            if j == 0:
                V(lambda e, slot=slot: e.memset(slot[:, :, 0:1], 0.0))
            else:
                prev = z2v((j - 1) % 2)
                V(lambda e, slot=slot, prev=prev: e.tensor_copy(
                    out=slot[:, :, 0:1], in_=prev[:, :, 512:513]))
                V(lambda e, slot=slot, prev=prev: e.tensor_copy(
                    out=prev[:, :, 513:514], in_=slot[:, :, 1:2]))
            if j == 3:
                V(lambda e, slot=slot: e.memset(slot[:, :, 513:514], 0.0))

        def conv_block(j):
            slot = z2v(j % 2)
            for fc in range(8):
                zc = slot[:, fc]
                w0, w1, w2 = (convwv[:, fc, k:k + 1] for k in range(3))
                bcv = convb[:, fc:fc + 1]
                V(lambda e, zc=zc, w1=w1, bcv=bcv: e.tensor_scalar(
                    out=CVT0[:, :], in0=zc[:, 1:513], scalar1=w1, scalar2=bcv,
                    op0=ALU.mult, op1=ALU.add))
                V(lambda e, zc=zc, w0=w0: e.tensor_scalar(
                    out=CVT1[:, :], in0=zc[:, 0:512], scalar1=w0, scalar2=None,
                    op0=ALU.mult, op1=ALU.bypass))
                V(lambda e: e.tensor_tensor(out=CVT0[:, :], in0=CVT0[:, :], in1=CVT1[:, :],
                                            op=ALU.add))
                V(lambda e, zc=zc, w2=w2: e.tensor_scalar(
                    out=CVT1[:, :], in0=zc[:, 2:514], scalar1=w2, scalar2=None,
                    op0=ALU.mult, op1=ALU.bypass))
                V(lambda e, fc=fc, j=j: e.tensor_tensor(
                    out=ufmv()[:, fc, j * 512:(j + 1) * 512],
                    in0=CVT0[:, :], in1=CVT1[:, :], op=ALU.add))

        produce_z2(0)
        for j in range(4):
            if j < 3:
                produce_z2(j + 1)
            conv_block(j)
        V_U = p.now("VS")
        vg1 = V(lambda e: e.tensor_scalar(out=ctxb[:, :], in0=ctxa[:, :], scalar1=1.0 / S,
                                          scalar2=None, op0=ALU.mult, op1=ALU.bypass))
        tg = None
        for c in range(8):
            tg = T(lambda e, c=c: e.matmul(
                psC[0:1, 640:643], ctxb[:, c:c + 1], wkg.rearrange("p (c k) -> p c k", c=8, k=3)[:, c],
                start=(c == 0), stop=(c == 7), skip_group_check=True),
                waits=[(VS, vg1)] if c == 0 else [])
        lg = p.dma("sync", sc1[0:1, 12:15], io["bkgen"], inc=LD)
        vg2 = V(lambda e: e.tensor_copy(out=sc1[0:1, 16:19], in_=psC[0:1, 640:643]),
                waits=[(TS, tg)])
        pad()
        vg3 = V(lambda e: e.tensor_tensor(out=sc1[0:1, 16:19], in0=sc1[0:1, 16:19],
                                          in1=sc1[0:1, 12:15], op=ALU.add), waits=[(LD, lg)])
        sg1 = SC(lambda e: e.activation(sc1[0:1, 20:23], sc1[0:1, 16:19], AF.Sigmoid,
                                        bias=0.0, scale=1.0), waits=[(VS, vg3)])
        vg4 = V(lambda e: e.tensor_reduce(out=sc1[0:1, 24:25], in_=sc1[0:1, 20:23], axis=AX.X,
                                          op=ALU.add), waits=[(SS, sg1)])
        pad()
        vg5 = V(lambda e: e.tensor_scalar(out=sc1[0:1, 25:26], in0=sc1[0:1, 24:25],
                                          scalar1=1.0 / 24.0, scalar2=None, op0=ALU.mult,
                                          op1=ALU.bypass))
        vg6 = V(lambda e: e.tensor_scalar(out=sc1[0:1, 26:27], in0=sc1[0:1, 24:25],
                                          scalar1=1.0 / 3.0, scalar2=None, op0=ALU.mult,
                                          op1=ALU.bypass))
        vg6 = pad()
        v_qs = bcast128(gvec[:, 0:1], sc1[0:1, 25:26], vg5, psC[:, 644:645])
        v_ks = bcast128(gvec[:, 1:2], sc1[0:1, 26:27], vg6, psC[:, 645:646])

        def build_wT(w_io, r0, nrows, extra_waits=()):
            l = p.dma("sync", FSTG[0:nrows, 0:HID], w_io[r0:r0 + nrows, :],
                      waits=[(VS, p.now("VS"))], inc=LD)
            vv = V(lambda e, nrows=nrows: e.tensor_copy(out=QITM[0:nrows, :],
                                                        in_=FSTG[0:nrows, 0:HID]),
                   waits=[(LD, l)] + list(extra_waits))
            tt = None
            for fc in range(8):
                tt = T(lambda e, fc=fc, nrows=nrows: e.transpose(
                    psTb[:, fc * nrows:(fc + 1) * nrows], QITM[0:nrows, fc * P:(fc + 1) * P],
                    identb[0:nrows, 0:nrows]),
                    waits=[(VS, vv), (VS, p.now("VS"))] if fc == 0 else [])
            return V(lambda e, nrows=nrows: e.tensor_copy(out=wslice[:, 0:8 * nrows],
                                                          in_=psTb[:, 0:8 * nrows]),
                     waits=[(TS, tt), (LD, p.now("LD"))])

        def project(nrows, dst_rows, scale_col, evac_to, wait_w):
            tprev = None
            for tb in range(4):
                dstp = psA if tb % 2 == 0 else psB
                half = (tb // 2) * 512
                for fc in range(8):
                    tprev = T(lambda e, tb=tb, fc=fc, dstp=dstp, half=half, nrows=nrows: e.matmul(
                        dstp[0:nrows, half:half + 512],
                        wslice[:, fc * nrows:(fc + 1) * nrows],
                        ufmv()[:, fc, tb * 512:(tb + 1) * 512],
                        start=(fc == 0), stop=(fc == 7), skip_group_check=True),
                        waits=[(VS, wait_w)] if (tb == 0 and fc == 0) else [])
            vlast = None
            for tb in range(4):
                dstp = psA if tb % 2 == 0 else psB
                half = (tb // 2) * 512
                vlast = V(lambda e, tb=tb, dstp=dstp, half=half: e.tensor_scalar(
                    out=evac_to[dst_rows, tb * 512:(tb + 1) * 512],
                    in0=dstp[0:(dst_rows.stop - dst_rows.start), half:half + 512],
                    scalar1=gvec[0:(dst_rows.stop - dst_rows.start), scale_col:scale_col + 1],
                    scalar2=None, op0=ALU.mult, op1=ALU.bypass),
                    waits=[(TS, tprev), (VS, p.now("VS"))] if tb == 0 else [])
            return vlast

        def build_kv(kv):
            vK = build_wT(io["wk"], kv * 64, 64, extra_waits=[(VS, p.now("VS"))])
            vKd = project(64, slice(0, 64), 1, kdup, vK)
            dK1 = p.dma("sync", kdup[64:128, :], kdup[0:64, :], waits=[(VS, vKd)], inc=LD)
            dK2 = p.dma("sync", khat[0:64, :], kdup[0:64, :], inc=LD)
            vV = build_wT(io["wv"], kv * 64, 64, extra_waits=[(VS, vKd)])
            vV2 = project(64, slice(0, 64), 1, vfmt, vV)
            tv = None
            for kpi in range(NKP):
                tv = T(lambda e, kpi=kpi: e.transpose(
                    psTb[:, kpi * 64:(kpi + 1) * 64], vfmt[0:64, kpi * P:(kpi + 1) * P],
                    identb[0:64, 0:64]),
                    waits=[(VS, vV2), (VS, p.now("VS"))] if kpi == 0 else [])
            vvk = V(lambda e: e.tensor_copy(out=vkt[:, :], in_=psTb[:, 0:NKP * 64]),
                    waits=[(TS, tv)])
            sKq = SC(lambda e: e.activation(QSQ[0:64, :], kdup[0:64, :], AF.Square,
                                            bias=0.0, scale=1.0),
                     waits=[(VS, vKd), (LD, p.now("LD")), (SS, p.now("SS"))])
            vkm = V(lambda e: e.memset(sc1[0:1, 28:29], 0.0))
            for tb in range(4):
                tn = T(lambda e, tb=tb: e.matmul(
                    psC[0:1, 0:512], ind64[:, :], QSQ[0:64, tb * 512:(tb + 1) * 512],
                    start=True, stop=True, skip_group_check=True),
                    waits=[(SS, sKq), (VS, p.now("VS"))] if tb == 0 else [(VS, p.now("VS"))])
                V(lambda e: e.tensor_reduce(out=sc1[0:1, 29:30], in_=psC[0:1, 0:512],
                                            axis=AX.X, op=ALU.max), waits=[(TS, tn)])
                pad()
                V(lambda e: e.tensor_tensor(out=sc1[0:1, 28:29], in0=sc1[0:1, 28:29],
                                            in1=sc1[0:1, 29:30], op=ALU.max))
                vkm = pad()
            V(lambda e: e.memset(QSQ[0:1, 0:S], 0.0), waits=[(TS, p.now("TS"))])
            V(lambda e: e.memset(ETL[0][0:1, 0:1024], 0.0))
            V(lambda e: e.memset(ETL[1][0:1, 0:1024], 0.0))
            vmk = None
            for kpi in range(NKP):
                V(lambda e, kpi=kpi: e.memset(QSQ[0:1, kpi * P:kpi * P + 64], 1.0))
                c0 = kpi * P + 64
                ei, off = (0, c0) if c0 < 1024 else (1, c0 - 1024)
                vmk = V(lambda e, ei=ei, off=off: e.memset(ETL[ei][0:1, off:off + 64], 1.0))
            dm1 = p.dma("sync", khat[64:65, :], QSQ[0:1, 0:S], waits=[(VS, vmk)], inc=LD)
            dm2a = p.dma("sync", khat[65:66, 0:1024], ETL[0][0:1, 0:1024], inc=LD)
            dmask = p.dma("sync", khat[65:66, 1024:2048], ETL[1][0:1, 0:1024], inc=LD)
            return vvk, vkm, dmask

        for pr in range(NPAIR):
            kv = pr // 2
            if pr % 2 == 0:
                vvk, v_km, dmask_kv = build_kv(kv)
            vWq = build_wT(io["wq"], pr * P, P, extra_waits=[(VS, p.now("VS"))])
            vqp = project(P, slice(0, P), 0, qpair, vWq)
            sQq = SC(lambda e: e.activation(QSQ[:, :], qpair[:, :], AF.Square, bias=0.0,
                                            scale=1.0), waits=[(VS, vqp), (SS, p.now("SS"))])
            vqm = V(lambda e: e.memset(sc[0:2, 31:32], 0.0))
            for tb in range(4):
                tn = T(lambda e, tb=tb: e.matmul(
                    psC[0:2, 0:512], indAB[:, :], QSQ[:, tb * 512:(tb + 1) * 512],
                    start=True, stop=True, skip_group_check=True),
                    waits=[(SS, sQq), (VS, p.now("VS"))] if tb == 0 else [(VS, p.now("VS"))])
                V(lambda e: e.tensor_reduce(out=sc[0:2, 30:31], in_=psC[0:2, 0:512],
                                            axis=AX.X, op=ALU.max), waits=[(TS, tn)])
                pad()
                V(lambda e: e.tensor_tensor(out=sc[0:2, 31:32], in0=sc[0:2, 31:32],
                                            in1=sc[0:2, 30:31], op=ALU.max))
                vqm = pad()
            tq = T(lambda e: e.transpose(psC[0:1, 0:P], sc[:, 31:32], ident[:, :]),
                   waits=[(VS, vqm)])
            vM1 = V(lambda e: e.tensor_scalar(out=sc1[0:1, 4:6], in0=psC[0:1, 0:2],
                                              scalar1=sc1[0:1, 28:29], scalar2=None,
                                              op0=ALU.add, op1=ALU.bypass),
                    waits=[(TS, tq), (VS, v_km)])
            pad()
            vM2 = V(lambda e: e.tensor_scalar(out=sc1[0:1, 4:6], in0=sc1[0:1, 4:6],
                                              scalar1=-1.0 / 16.0, scalar2=None,
                                              op0=ALU.mult, op1=ALU.bypass))
            vM2 = pad()
            vb0 = bcast128(ebias[:, 0:1], sc1[0:1, 4:5], vM2, psC[:, 646:647], wait_s=p.now("SS"))
            vb1 = bcast128(ebias[:, 1:2], sc1[0:1, 5:6], vM2, psC[:, 647:648], wait_s=p.now("SS"))
            V(lambda e: e.tensor_tensor(out=ebias[:, 2:3], in0=ebias[:, 0:1],
                                        in1=ln32t[:, :], op=ALU.subtract),
              waits=[(VS, max(vb0, vb1)), (GD, G_CONST)])
            vb3 = V(lambda e: e.tensor_tensor(out=ebias[:, 3:4], in0=ebias[:, 1:2],
                                              in1=ln32t[:, :], op=ALU.subtract))
            vb3 = pad()
            for h01 in range(2):
                src = qpair[64 * h01:64 * (h01 + 1), :]
                for par in range(2):
                    p.dma("sync", QHAT[2 * h01 + par][0:64, :], src, waits=[(VS, vqp)], inc=LD)
            LD_Q = p.now("LD")

            EBUF = [E1A, E1B, ETL[0], ETL[1]]
            for qh in range(2):
                q0 = qh * 1024
                exp_done = {}

                def emit_E(n):
                    kp, h01 = n // 2, n % 2
                    g = 2 * h01 + (kp % 2)
                    mt = maskE[:, (kp // 2) * 32:(kp // 2) * 32 + 32]
                    e1 = EBUF[n % 4]
                    for qb in range(2):
                        T(lambda e, g=g, mt=mt, e1=e1, qb=qb, kp=kp: e.matmul(
                            psD[32 * g:32 * (g + 1), qb * 512:(qb + 1) * 512], mt,
                            e1[:, qb * 512:(qb + 1) * 512],
                            start=(kp < 2), stop=(kp >= NKP - 2),
                            tile_position=(0, 32 * g), skip_group_check=True),
                            waits=([(SS, exp_done[n])] if qb == 0 else []) +
                                  ([(VS, p.now("VS"))] if n == 0 and qb == 0 else []))

                for n in range(2 * NKP):
                    kp, h01 = n // 2, n % 2
                    if n >= 3:
                        emit_E(n - 3)
                    slp = PSLOT[n % 3]
                    r0, r1 = 64 * h01, 64 * (h01 + 1)
                    tpos = (64 * h01, 0)
                    wa = [(LD, LD_Q), (VS, p.now("VS"))] if n <= 2 else []
                    T(lambda e, kp=kp, slp=slp, q0=q0, r0=r0, r1=r1, tpos=tpos: e.matmul(
                        slp[:, 0:512], kdup[r0:r1, kp * P:(kp + 1) * P],
                        qpair[r0:r1, q0:q0 + 512],
                        start=True, stop=True, tile_position=tpos, skip_group_check=True),
                        waits=wa)
                    t1 = T(lambda e, kp=kp, slp=slp, q0=q0, r0=r0, r1=r1, tpos=tpos: e.matmul(
                        slp[:, 512:1024], kdup[r0:r1, kp * P:(kp + 1) * P],
                        qpair[r0:r1, q0 + 512:q0 + 1024],
                        start=True, stop=True, tile_position=tpos, skip_group_check=True))
                    exp_done[n] = SC(lambda e, slp=slp, h01=h01, n=n: e.activation(
                        EBUF[n % 4][:, :], slp[:, :], AF.Exp,
                        bias=ebias[:, h01:h01 + 1], scale=1.0),
                        waits=[(TS, t1), (VS, vb3)] if n == 0 else [(TS, t1)])
                for n in range(2 * NKP - 3, 2 * NKP):
                    emit_E(n)
                T_E = p.now("TS")
                sLn = SC(lambda e: e.activation(LNE[:, :], psD[:, :], AF.Ln, bias=0.0, scale=1.0),
                         waits=[(TS, T_E)])
                V(lambda e, q0=q0: e.tensor_scalar(out=bhi[:, q0:q0 + 1024], in0=LNE[:, :],
                                                   scalar1=-1.0, scalar2=None,
                                                   op0=ALU.mult, op1=ALU.bypass),
                  waits=[(SS, sLn)])
            V_BHI = p.now("VS")

            for qh in range(2):
                q0 = qh * 1024
                prev_s2 = {}
                exp2_done = {}
                db_done = {}
                av_t = None

                def emit_db(n):
                    kp, h01 = n // 2, n % 2
                    reg_i = 2 * h01 + (kp % 2)
                    g = reg_i
                    db_done[n] = p.dma(
                        "sync", QHAT[reg_i][64:66, :],
                        bhi[32 * g + 2 * (kp // 2):32 * g + 2 * (kp // 2) + 2, :],
                        waits=[(VS, V_BHI), (TS, prev_s2.get(reg_i, 0))], inc=LD)

                def emit_av(n):
                    kp, h01 = n // 2, n % 2
                    etl = EBUF[n % 4]
                    for qb in range(2):
                        nonlocal_av = T(lambda e, kp=kp, h01=h01, qb=qb, etl=etl: e.matmul(
                            psD[64 * h01:64 * (h01 + 1), qb * 512:(qb + 1) * 512],
                            vkt[:, kp * 64:(kp + 1) * 64],
                            etl[:, qb * 512:(qb + 1) * 512],
                            start=(kp == 0), stop=(kp == NKP - 1),
                            tile_position=(0, 64 * h01), skip_group_check=True),
                            waits=([(SS, exp2_done[n])] if qb == 0 else []) +
                                  ([(VS, p.now("VS"))] if n == 0 and qb == 0 else []))
                    return nonlocal_av

                emit_db(0)
                emit_db(1)
                for n in range(2 * NKP):
                    kp, h01 = n // 2, n % 2
                    reg_i = 2 * h01 + (kp % 2)
                    reg = QHAT[reg_i]
                    if n + 2 < 2 * NKP:
                        emit_db(n + 2)
                    if n >= 2:
                        av_t = emit_av(n - 2)
                    slp = PSLOT[n % 3]
                    T(lambda e, kp=kp, reg=reg, slp=slp, q0=q0: e.matmul(
                        slp[:, 0:512], khat[:, kp * P:(kp + 1) * P], reg[:, q0:q0 + 512],
                        start=True, stop=True, skip_group_check=True),
                        waits=[(LD, db_done[n])] + ([(VS, p.now("VS"))] if n == 0 else []))
                    t1 = T(lambda e, kp=kp, reg=reg, slp=slp, q0=q0: e.matmul(
                        slp[:, 512:1024], khat[:, kp * P:(kp + 1) * P],
                        reg[:, q0 + 512:q0 + 1024],
                        start=True, stop=True, skip_group_check=True))
                    prev_s2[reg_i] = t1
                    exp2_done[n] = SC(lambda e, slp=slp, h01=h01, n=n: e.activation(
                        EBUF[n % 4][:, :], slp[:, :], AF.Exp,
                        bias=ebias[:, 2 + h01:3 + h01], scale=1.0), waits=[(TS, t1)])
                for n in range(2 * NKP - 2, 2 * NKP):
                    av_t = emit_av(n)
                va = V(lambda e: e.tensor_copy(out=ASTG[:, :], in_=psD[:, :]),
                       waits=[(TS, av_t), (LD, p.now("LD"))])
                p.dma("sync", aspill.ap()[(b * NPAIR + pr) * P:(b * NPAIR + pr + 1) * P,
                                          q0:q0 + 1024],
                      ASTG[:, :], waits=[(VS, va)], inc=LD)
        LD_ATTN = p.now("LD")

        for r in range(8):
            l = p.dma("sync", FSTG[:, :], io["wo"][r * P:(r + 1) * P, :],
                      waits=([(LD, LD_ATTN)] if r == 0 else []) + [(VS, p.now("VS"))], inc=LD)
            vv = V(lambda e: e.tensor_copy(out=QITM[:, :], in_=FSTG[:, :]),
                   waits=[(LD, l), (VS, p.now("VS"))])
            tt = None
            for fc in range(8):
                tt = T(lambda e, fc=fc: e.transpose(psTb[:, fc * P:(fc + 1) * P],
                                                    QITM[:, fc * P:(fc + 1) * P], identb[:, :]),
                       waits=[(VS, vv), (VS, p.now("VS"))] if fc == 0 else [])
            V(lambda e, r=r: e.tensor_copy(
                out=WTv()[:, :, r * P:(r + 1) * P],
                in_=psTb[:, 0:1024].rearrange("p (c n) -> p c n", c=8, n=P)),
                waits=[(TS, tt)])
        V_WO = p.now("VS")

        def obank(oc):
            return [psA, psB, psC, psD][oc // 2][:, (oc % 2) * 512:(oc % 2 + 1) * 512]

        for qb in range(4):
            lds = None
            for pr in range(NPAIR):
                lds = p.dma("sync", OSTG[:, pr * 512:(pr + 1) * 512],
                            aspill.ap()[(b * NPAIR + pr) * P:(b * NPAIR + pr + 1) * P,
                                        qb * 512:(qb + 1) * 512],
                            waits=[(VS, V_WO)] if (qb == 0 and pr == 0) else [], inc=LD)
            tt = None
            for pr in range(NPAIR):
                for oc in range(8):
                    tt = T(lambda e, pr=pr, oc=oc: e.matmul(
                        obank(oc), WTv()[:, pr, oc * P:(oc + 1) * P],
                        OSTG[:, pr * 512:(pr + 1) * 512],
                        start=(pr == 0), stop=(pr == NPAIR - 1), skip_group_check=True),
                        waits=([(LD, lds), (VS, V_WO), (VS, p.now("VS"))]
                               if (pr == 0 and oc == 0) else []))
            for oc in [7, 6, 0, 1, 2, 3, 4, 5]:
                vv = V(lambda e, oc=oc: e.tensor_copy(out=ASTG[:, 0:512], in_=obank(oc)),
                       waits=[(TS, tt)] if oc == 7 else [(TS, p.now("TS"))])
                t2 = None
                for blk in range(4):
                    t2 = T(lambda e, blk=blk: e.transpose(
                        psTb[:, 1024 + blk * P:1024 + (blk + 1) * P],
                        ASTG[:, blk * P:(blk + 1) * P], identb[:, :]),
                        waits=[(VS, vv)] if blk == 0 else [])
                for blk in range(4):
                    t = TPB * b + qb * 4 + blk
                    x_t = xt(t)
                    V(lambda e, oc=oc, blk=blk, x_t=x_t: e.tensor_tensor(
                        out=x_t[:, oc * P:(oc + 1) * P], in0=x_t[:, oc * P:(oc + 1) * P],
                        in1=psTb[:, 1024 + blk * P:1024 + (blk + 1) * P], op=ALU.add),
                        waits=[(TS, t2)] if blk == 0 else [])

    if "f1" in stages:
        ffn_pass(0)
    if "a0" in stages:
        attention(0)
    if "a1" in stages:
        attention(1)
    if "f2" in stages:
        ffn_pass(1)
    V_END = p.now("VS")
    if debug:
        dbgq = nc.dram_tensor("dbgq", [P, 2048], F32, kind="ExternalOutput").ap()
        p.dma("sync", dbgq[:, 0:1024], UF32[:, 3072:4096], waits=[(VS, V_END)], inc=LD)
        dbq2 = V(lambda e: e.tensor_copy(out=UF32[:, 5120:6144], in_=QI8))
        p.dma("sync", dbgq[:, 1024:2048], UF32[:, 5120:6144], waits=[(VS, dbq2)], inc=LD)
        dbg = nc.dram_tensor("dbg", [P, 192], F32, kind="ExternalOutput").ap()
        p.dma("sync", dbg[:, 0:32], sc[:, :], waits=[(VS, V_END)], inc=LD)
        p.dma("sync", dbg[0:1, 32:64], sc1[:, :], inc=LD)
        p.dma("sync", dbg[:, 64:96], statm[:, :], inc=LD)
        p.dma("sync", dbg[:, 96:128], statr[:, :], inc=LD)
        p.dma("sync", dbg[:, 128:160], stata[:, :], inc=LD)
        p.dma("sync", dbg[:, 160:192], statz[:, :], inc=LD)
    for t in range(NT):
        p.dma("sync", yio[t], xt(t), waits=[(VS, V_END)] if t == 0 else [], inc=LD)
    LD_END = p.now("LD")
    p.emit("sync", lambda e: e.nop(), waits=[(LD, LD_END)])

    p.build(stack)
    stack.close()
    return nc


_NC_CACHE = None


def kernel(L0_attn_hidden_states, L0_attn_w_q, L0_attn_w_k, L0_attn_w_v, L0_attn_w_o,
           L0_attn_w_kgen, L0_attn_b_kgen, L0_attn_w_conv, L0_attn_b_conv,
           L0_ffn_weight, L0_ffn_bias):
    global _NC_CACHE
    if _NC_CACHE is None:
        _NC_CACHE = build_nc()
    nc = _NC_CACHE
    x = np.ascontiguousarray(L0_attn_hidden_states, dtype=np.float32)
    shared = {
        "wq": np.ascontiguousarray(L0_attn_w_q, dtype=np.float32),
        "wk": np.ascontiguousarray(L0_attn_w_k, dtype=np.float32),
        "wv": np.ascontiguousarray(L0_attn_w_v, dtype=np.float32),
        "wo": np.ascontiguousarray(L0_attn_w_o, dtype=np.float32),
        "wkgen": np.ascontiguousarray(L0_attn_w_kgen, dtype=np.float32),
        "bkgen": np.ascontiguousarray(L0_attn_b_kgen, dtype=np.float32).reshape(1, 3),
        "wconv": np.ascontiguousarray(L0_attn_w_conv, dtype=np.float32).reshape(HID, 3),
        "bconv": np.ascontiguousarray(L0_attn_b_conv, dtype=np.float32).reshape(1, HID),
        "wffn": np.ascontiguousarray(L0_ffn_weight, dtype=np.float32),
        "bffn": np.ascontiguousarray(L0_ffn_bias, dtype=np.float32).reshape(1, HID),
    }
    in_maps = []
    for c in range(8):
        m = dict(shared)
        m["x"] = np.ascontiguousarray(x[2 * c:2 * c + 2].reshape(B * S, HID))
        in_maps.append(m)
    res = run_bass_kernel_spmd(nc, in_maps, core_ids=list(range(8)))
    out = np.concatenate([np.asarray(r["y"]).reshape(B, S, HID) for r in res.results], axis=0)
    return out.astype(np.float32)
